# version 18
# baseline (speedup 1.0000x reference)
import math
import numpy as np

import concourse.bass as bass
import concourse.mybir as mybir
from concourse.bass_utils import run_bass_kernel_spmd

DEBUG = False

F32 = mybir.dt.float32
F16 = mybir.dt.float16
I32 = mybir.dt.int32
AF = mybir.ActivationFunctionType
ALU = mybir.AluOpType
AX = mybir.AxisListType

NCORES = 8
E = 2048
HDA = 128
QL, KL = 1536, 512
DN, DR, DV = 128, 64, 128
DQ = DN + DR
FF = 8192
SC_A = 1.0 / math.sqrt(HDA)
SC_B = 1.0 / math.sqrt(DQ)
ROPE_BASE = 10000.0
SAFE = 1.02
RMAGIC = 12582912.0
EXPOFF = 60.0

ENGINES = ("tensor", "vector", "scalar", "gpsimd", "sync")


class S:
    __slots__ = ("h", "n", "name")

    def __init__(self, h, name):
        self.h = h
        self.n = 0
        self.name = name


class PB:
    def __init__(self, nc):
        self.nc = nc
        self.q = {e: [] for e in ENGINES}
        self._waited = {e: {} for e in ENGINES}

    def sem(self, name):
        return S(self.nc.alloc_semaphore(name), name)

    def emit(self, eng, fn, wait=(), inc=()):
        w = []
        for s, v in wait:
            if v > self._waited[eng].get(id(s), -1):
                self._waited[eng][id(s)] = v
                w.append((s.h, v))
        ii = []
        for s, k in inc:
            s.n += k
            ii.append((s.h, k))
        self.q[eng].append((fn, w, ii))

    def dma(self, eng, out, in_, sem, wait=(), fence=False):
        if fence:
            wait = [(sem, sem.n)] + list(wait)
        self.emit(eng, lambda E_, o=out, i=in_: E_.dma_start(out=o, in_=i),
                  wait=wait, inc=[(sem, 16)])
        return sem.n

    def dma_slow(self, eng, out, in_, sem, wait=()):
        self.emit(eng, lambda E_, o=out, i=in_: E_.dma_start(
            out=o, in_=i, allow_slow_non_contiguous=True),
            wait=wait, inc=[(sem, 16)])
        return sem.n

    def dmaT(self, eng, out, in_, sem, wait=(), fence=False):
        if fence:
            wait = [(sem, sem.n)] + list(wait)
        self.emit(eng, lambda E_, o=out, i=in_: E_.dma_start_transpose(o, i),
                  wait=wait, inc=[(sem, 16)])
        return sem.n

    def finalize(self):
        with self.nc.Block() as block:
            for eng in ENGINES:
                stream = self.q[eng]
                if not stream:
                    continue
                getattr(block, eng)(_mk_runner(stream))


def _mk_runner(stream):
    def run(Eng):
        for fn, w, ii in stream:
            for h, v in w:
                Eng.wait_ge(h, v)
            ins = fn(Eng)
            for h, k in ii:
                ins.then_inc(h, k)
    return run


def _rope_tables(S_total):
    inv_freq = 1.0 / ROPE_BASE ** (np.arange(0, DR, 2, dtype=np.float64) / DR)
    t = np.arange(S_total, dtype=np.float64)
    fr = np.outer(inv_freq, t)
    return np.cos(fr).astype(np.float32), np.sin(fr).astype(np.float32)


def host_inputs(inp, S_total):
    TOK = S_total // NCORES
    x = np.asarray(inp["L0_attn_a_hidden_states"]).reshape(S_total, E)
    W_in = np.asarray(inp["L0_attn_a_W_in"])
    b_in = np.asarray(inp["L0_attn_a_b_in"])
    W_out = np.asarray(inp["L0_attn_a_W_out"])
    b_out = np.asarray(inp["L0_attn_a_b_out"])
    wqa = np.asarray(inp["L0_attn_b_w_q_a"])
    qaln = np.asarray(inp["L0_attn_b_q_a_ln"])
    wqb = np.asarray(inp["L0_attn_b_w_q_b"])
    wkva = np.asarray(inp["L0_attn_b_w_kv_a"])
    kvln = np.asarray(inp["L0_attn_b_kv_a_ln"])
    wkvb = np.asarray(inp["L0_attn_b_w_kv_b"])
    woutb = np.asarray(inp["L0_attn_b_w_out"])
    w1 = np.asarray(inp["L0_ffn_w1"])
    w2 = np.asarray(inp["L0_ffn_w2"])

    WinT = np.ascontiguousarray(W_in.T)
    WoutT = np.ascontiguousarray(W_out.T)
    perm = np.concatenate([np.arange(0, DR, 2), np.arange(1, DR, 2)])
    c32, s32 = _rope_tables(S_total)
    cosq = np.ascontiguousarray(np.concatenate([c32, c32, c32, c32], axis=0))
    sinq = np.ascontiguousarray(np.concatenate([-s32, s32, -s32, s32], axis=0))

    maps = []
    for c in range(NCORES):
        h0, h1 = 2 * c, 2 * c + 1
        cols = np.concatenate([base + np.arange(HDA * h0, HDA * (h1 + 1))
                               for base in (0, E, 2 * E)])
        winT_own = np.ascontiguousarray(WinT[:, cols])
        bin_own = np.ascontiguousarray(b_in[cols])

        def qb_cols(h):
            base = DQ * h
            return (base + np.arange(DN), base + DN + perm[:32], base + DN + perm[32:])

        n0, l0, hh0 = qb_cols(h0)
        n1, l1, hh1 = qb_cols(h1)
        pe = np.concatenate([l0, hh0, l1, hh1])
        pesw = np.concatenate([hh0, l0, hh1, l1])
        wqb_own = np.ascontiguousarray(wqb[:, np.concatenate([n0, n1, pe, pesw])])

        def kvb_cols(h):
            base = (DN + DV) * h
            return base + np.arange(DN), base + DN + np.arange(DV)

        kn0, v0 = kvb_cols(h0)
        kn1, v1 = kvb_cols(h1)
        wkvb_own = np.ascontiguousarray(wkvb[:, np.concatenate([kn0, kn1, v0, v1])])
        pe_cols = KL + perm
        pesw_cols = KL + np.concatenate([perm[32:], perm[:32]])
        wkva_own = np.ascontiguousarray(
            np.concatenate([wkva[:, :KL], wkva[:, pe_cols], wkva[:, pesw_cols]], axis=1))
        ts = slice(TOK * c, TOK * (c + 1))
        maps.append({
            "x_own": np.ascontiguousarray(x[ts]),
            "winT_own": winT_own,
            "bin_own": bin_own,
            "woutT_own": np.ascontiguousarray(WoutT[256 * c:256 * (c + 1)]),
            "b_out": np.ascontiguousarray(b_out),
            "wqa": np.ascontiguousarray(wqa),
            "qaln": np.ascontiguousarray(qaln),
            "wqb_own": wqb_own,
            "wkva_own": wkva_own,
            "kvln": np.ascontiguousarray(kvln),
            "wkvb_own": wkvb_own,
            "woutb_own": np.ascontiguousarray(woutb[256 * c:256 * (c + 1)]),
            "w1_own": np.ascontiguousarray(w1[(FF // NCORES) * c:(FF // NCORES) * (c + 1)]),
            "w2_own": np.ascontiguousarray(w2[(E // NCORES) * c:(E // NCORES) * (c + 1)]),
            "cosq": cosq,
            "sinq": sinq,
            "cosk_own": np.ascontiguousarray(np.concatenate([c32[:, ts], c32[:, ts]], axis=0)),
            "sink_own": np.ascontiguousarray(np.concatenate([-s32[:, ts], s32[:, ts]], axis=0)),
        })
    return maps


def build(S_total):
    TOK = S_total // NCORES
    NT = TOK // 128
    NTH = TOK // 512
    NKT = S_total // 128
    NQB = S_total // 512
    NTT = S_total // 512
    FSH = FF // NCORES
    ESH = E // NCORES
    assert TOK % 512 == 0 and NKT * 256 <= 16384

    nc = bass.Bass("TRN2", target_bir_lowering=False, debug=False)

    din = lambda n, sh: nc.dram_tensor(n, sh, F32, kind="ExternalInput").ap()
    x_in = din("x_own", [TOK, E])
    winT = din("winT_own", [E, 768])
    bin_d = din("bin_own", [768])
    woutT_d = din("woutT_own", [256, E])
    bout_d = din("b_out", [E])
    wqa_d = din("wqa", [E, QL])
    qaln_d = din("qaln", [QL])
    wqb_d = din("wqb_own", [QL, 512])
    wkva_d = din("wkva_own", [E, KL + 2 * DR])
    kvln_d = din("kvln", [KL])
    wkvb_d = din("wkvb_own", [KL, 512])
    woutb_d = din("woutb_own", [256, E])
    w1o_d = din("w1_own", [FSH, E])
    w2o_d = din("w2_own", [ESH, FF])
    cosq_d = din("cosq", [128, S_total])
    sinq_d = din("sinq", [128, S_total])
    cosk_d = din("cosk_own", [64, TOK])
    sink_d = din("sink_own", [64, TOK])
    y_out = nc.dram_tensor("y", [TOK, E], F32, kind="ExternalOutput").ap()

    dint = lambda n, sh: nc.dram_tensor(n, sh, F16).ap()
    dsh = lambda n, sh: nc.dram_tensor(n, sh, F16, addr_space="Shared").ap()
    NTH_ = max(1, TOK // 512)
    ag1c_t = [dint(f"ag1c{t}", [E, 512]) for t in range(NTH_)]
    ag1o_t = [dsh(f"ag1o{t}", [NCORES * E, 512]) for t in range(NTH_)]
    AG3R = QL + KL + DR
    ag3c_t = [dint(f"ag3c{t}", [AG3R, 512]) for t in range(NTH_)]
    ag3o_t = [dsh(f"ag3o{t}", [NCORES * AG3R, 512]) for t in range(NTH_)]
    w1pc = dint("w1pc", [FSH, E])
    w1po = dsh("w1po", [FF, E])
    w2pc = dint("w2pc", [ESH, FF])
    w2po = dsh("w2po", [E, FF])
    qa_dr = dint("qa_dr", [256, S_total])
    qb_dr = dint("qb_dr", [384, S_total])
    atnA_dr = dint("atnA_dr", [256, S_total])
    atnB_dr = dint("atnB_dr", [256, S_total])
    p2h = [dint("p2a", [S_total // 2, E]), dint("p2b", [S_total // 2, E])]
    rs2h = [dint("rs2a", [TOK // 2, E]), dint("rs2b", [TOK // 2, E])]
    p4h = [dint("p4a", [S_total // 2, E]), dint("p4b", [S_total // 2, E])]
    rs4h = [dint("rs4a", [TOK // 2, E]), dint("rs4b", [TOK // 2, E])]

    LTm = np.zeros((512, 512), dtype=np.float16)
    for k in range(512):
        LTm[k, k:] = 1
    LT_d = nc.inline_tensor(LTm, name="LTm").ap()
    ID_d = nc.inline_tensor(np.eye(128, dtype=np.float16), name="IDm").ap()
    IDf_d = nc.inline_tensor(np.eye(128, dtype=np.float32), name="IDf").ap()

    p = PB(nc)
    SY = p.sem("SY")
    GP = p.sem("GP")
    MM = p.sem("MM")
    VE = p.sem("VE")
    SE = p.sem("SE")
    CC = p.sem("CC")
    SC = p.sem("SC")
    PL = p.sem("PL")
    SY2 = p.sem("SY2")
    GP2 = p.sem("GP2")

    sb = lambda n, sh, dt: nc.alloc_sbuf_tensor(n, sh, dt)
    x_tm = sb("x_tm", [128, NT, E], F32)
    GA = sb("GA", [128, 16384], F16)
    GB = sb("GB", [128, 16384], F16)
    GC = sb("GC", [128, 16384], F16)
    lt = sb("lt", [128, 4, 512], F16)
    idt = sb("idt", [128, 128], F16)
    idtf = sb("idtf", [128, 128], F32)
    onec = sb("onec", [128, 1], F16)
    onecf = sb("onecf", [128, 1], F32)
    oner16 = sb("oner16", [1, 128], F16)
    onerf = sb("onerf", [1, 128], F32)
    pacc = sb("pacc", [128, 512], F32)
    stats = sb("stats", [128, 24], F32)
    sm1 = sb("sm1", [1, 32], F32)
    bcol = sb("bcol", [128, 4], F32)
    rsum = sb("rsum", [1, 512], F32)
    rbc = sb("rbc", [128, 512], F32)
    fsc = sb("fsc", [128, 512], F32)
    s2f = sb("s2f", [128, 512], F32)
    rs2f = sb("rs2f", [128, 512], F32)
    pacc2 = s2f
    i32s = sb("i32s", [128, 512], I32)
    binsb = sb("binsb", [128, 6], F32)
    gainsb = sb("gainsb", [128, 16], F32)
    lnf = sb("lnf", [128, E], F32)
    hmax = sb("hmax", [128, 512], F32)
    sr8 = sb("sr8", [8, 128], F32)
    boutbc = sb("boutbc", [128, E], F16)
    wsc = sb("wsc", [128, 512], F16)
    wopb = sb("wopb", [128, 2, 2048], F16)

    ps_s = nc.alloc_psum_tensor("ps_s", [128, 2048], F32)
    ps_p = nc.alloc_psum_tensor("ps_p", [128, 2048], F32)
    pv_b = ps_p[:, 0:512]
    sum_b = ps_p[:, 512:1024]
    bc_b = ps_p[:, 1024:1536]
    tx_b = ps_p[:, 1536:2048]
    ps_s16 = ps_s[:, :].bitcast(F16)
    tx16 = ps_p[:, 1536:2048].bitcast(F16)

    def view(G, off, *shape):
        n = int(np.prod(shape))
        assert off + n <= 16384, (off, shape)
        v = G[:, off:off + n]
        if len(shape) == 1:
            return v
        names = " ".join(f"a{i}" for i in range(len(shape)))
        return v.rearrange(f"p ({names}) -> p {names}",
                           **{f"a{i}": s for i, s in enumerate(shape)})

    p.dma("sync", idt[:, :], ID_d[:, :], SY)
    p.dma("sync", idtf[:, :], IDf_d[:, :], SY)
    for j in range(4):
        p.dma("sync", lt[:, j, :], LT_d[128 * j:128 * (j + 1), :], SY)
    p.emit("vector", lambda E_: E_.memset(onec[:, :], 1.0), inc=[(VE, 1)])
    p.emit("vector", lambda E_: E_.memset(onecf[:, :], 1.0), inc=[(VE, 1)])
    p.emit("vector", lambda E_: E_.memset(oner16[:, :], 1.0), inc=[(VE, 1)])
    p.emit("vector", lambda E_: E_.memset(onerf[:, :], 1.0), inc=[(VE, 1)])
    p.emit("vector", lambda E_: E_.memset(sm1[:, :], 0.0), inc=[(VE, 1)])
    mcons = VE.n
    mx_t = []
    for i in range(NT):
        sem = SY if i % 2 == 0 else SY2
        mx_t.append((sem, p.dma("sync", x_tm[:, i, :], x_in[128 * i:128 * (i + 1), :],
                                sem, fence=True)))
    mx = [mx_t[-2], mx_t[-1]] if NT >= 2 else [mx_t[-1]]
    b16row = sb("b16row", [1, E], F16)
    p.dma("gpsimd", b16row[:, :], bout_d[None, :], GP)
    mb16 = GP.n
    for q4 in range(4):
        bsl = b16row[:, 512 * q4:512 * (q4 + 1)]
        p.emit("tensor", lambda E_, b=bsl: E_.matmul(
            bc_b[:, :], oner16[:, :], b, start=True, stop=True),
            wait=[(GP, mb16), (VE, max(mcons, VE.n)), (SY, SY.n)], inc=[(MM, 1)])
        dsl = boutbc[:, 512 * q4:512 * (q4 + 1)]
        p.emit("vector", lambda E_, d=dsl: E_.tensor_copy(d, bc_b[:, :]),
               wait=[(MM, MM.n)], inc=[(VE, 1)])
    m_boutbc = VE.n

    def dump_x(name):
        d = nc.dram_tensor(name, [TOK, E], F32, kind="ExternalOutput").ap()
        mve = VE.n
        for i in range(NT):
            p.dma("sync", d[128 * i:128 * (i + 1), :], x_tm[:, i, :], SY,
                  wait=[(VE, mve)])
        msy = SY.n
        p.emit("vector", lambda E_: E_.engine_nop(), wait=[(SY, msy)], inc=[(VE, 1)])

    def wquant_units(src, n_rows, width, dst, res):
        units = []
        for ch in range(n_rows // 128):
            def u_p1(ch=ch, g_=None):
                for g in ([g_] if g_ is not None else []):
                    src_sl = src[128 * ch:128 * (ch + 1), 2048 * g:2048 * (g + 1)]
                    mve_pre = VE.n
                    mld = p.dma("sync", lnf[:, :], src_sl, SC,
                                wait=[(VE, mve_pre), (SE, SE.n)] if ch == 0 and g == 0
                                else [(VE, mve_pre)],
                                fence=True)
                    p.emit("vector", lambda E_: E_.reduce_sum(
                        stats[:, 16:17], lnf[:, :], axis=AX.X, apply_absolute_value=True),
                        wait=[(SC, mld)], inc=[(VE, 1)])
                    if g == 0:
                        p.emit("vector", lambda E_: E_.tensor_copy(stats[:, 17:18], stats[:, 16:17]),
                               wait=[(VE, VE.n)], inc=[(VE, 1)])
                    else:
                        p.emit("vector", lambda E_: E_.tensor_add(
                            stats[:, 17:18], stats[:, 17:18], stats[:, 16:17]),
                            wait=[(VE, VE.n)], inc=[(VE, 1)])

            def u_sc(ch=ch):
                p.emit("vector", lambda E_, w=width: E_.tensor_scalar(
                    stats[:, 18:19], stats[:, 17:18], 1.0 / w, 1e-5, op0=ALU.mult, op1=ALU.max),
                    wait=[(VE, VE.n)], inc=[(VE, 1)])
                p.emit("vector", lambda E_: E_.reciprocal(stats[:, 19:20], stats[:, 18:19]),
                       wait=[(VE, VE.n)], inc=[(VE, 1)])

            def u_p2(ch=ch, g_=0):
                g = g_
                src_sl = src[128 * ch:128 * (ch + 1), 2048 * g:2048 * (g + 1)]
                mve_pre = VE.n
                mld = p.dma("sync", lnf[:, :], src_sl, SC, wait=[(VE, mve_pre)],
                            fence=True)
                for sl in range(4):
                    a = lnf[:, 512 * sl:512 * (sl + 1)]
                    mse_pre = SE.n
                    p.emit("vector", lambda E_, a=a: E_.tensor_scalar(
                        fsc[:, :], a, stats[:, 19:20], 1.4999999, op0=ALU.mult, op1=ALU.min),
                        wait=[(SC, mld), (VE, VE.n), (SE, mse_pre)],
                        inc=[(VE, 1)])
                    p.emit("vector", lambda E_: E_.tensor_scalar(
                        hmax[:, :], fsc[:, :], -1.4999999, RMAGIC, op0=ALU.max, op1=ALU.add),
                        wait=[(VE, VE.n)], inc=[(VE, 1)])
                    p.emit("vector", lambda E_: E_.tensor_scalar_add(
                        fsc[:, :], hmax[:, :], -RMAGIC),
                        wait=[(VE, VE.n)], inc=[(VE, 1)])
                    msc_pre = SC.n
                    p.emit("vector", lambda E_: E_.tensor_single_scalar(
                        wsc[:, :], fsc[:, :], stats[:, 18:19], op=ALU.mult),
                        wait=[(VE, VE.n), (SC, msc_pre)], inc=[(VE, 1)])
                    p.dma("sync",
                          dst[128 * ch:128 * (ch + 1),
                              2048 * g + 512 * sl:2048 * g + 512 * (sl + 1)],
                          wsc[:, :], SC, wait=[(VE, VE.n)])
                res["m"] = SC.n

            for g in range(width // 2048):
                units.append(lambda f=u_p1, ch=ch, g=g: f(ch, g))
            units.append(u_sc)
            for g in range(width // 2048):
                units.append(lambda f=u_p2, ch=ch, g=g: f(ch, g))
        return units

    def layernorm_tile(i, out16=None, mw=None, mxw=None):
        mxw = mx if mxw is None else mxw
        p.emit("vector", lambda E_, i=i: E_.reduce_sum(stats[:, 0:1], x_tm[:, i, :], axis=AX.X),
               wait=[(VE, mw if mw is not None else VE.n)] + list(mxw), inc=[(VE, 1)])
        scr = view(GB, 2048, 2048)
        p.emit("scalar", lambda E_, i=i, scr=scr: E_.activation(
            out=scr, in_=x_tm[:, i, :], func=AF.Square, accum_out=stats[:, 1:2]),
            wait=[(SE, SE.n), (VE, mw if mw is not None else VE.n)] + list(mxw), inc=[(SE, 1)])
        mse = SE.n
        p.emit("vector", lambda E_: E_.tensor_scalar_mul(stats[:, 2:3], stats[:, 0:1], 1.0 / E),
               wait=[(VE, VE.n), (SE, mse)], inc=[(VE, 1)])
        p.emit("vector", lambda E_: E_.tensor_scalar_mul(stats[:, 3:4], stats[:, 1:2], 1.0 / E),
               inc=[(VE, 1)])
        p.emit("vector", lambda E_: E_.tensor_tensor(stats[:, 4:5], stats[:, 2:3], stats[:, 2:3], op=ALU.mult),
               wait=[(VE, VE.n)], inc=[(VE, 1)])
        p.emit("vector", lambda E_: E_.scalar_tensor_tensor(
            stats[:, 5:6], stats[:, 3:4], 1e-5, stats[:, 4:5], op0=ALU.add, op1=ALU.subtract),
            wait=[(VE, VE.n)], inc=[(VE, 1)])
        p.emit("scalar", lambda E_: E_.activation(stats[:, 6:7], stats[:, 5:6], AF.Sqrt),
               wait=[(VE, VE.n)], inc=[(SE, 1)])
        p.emit("vector", lambda E_: E_.reciprocal(stats[:, 7:8], stats[:, 6:7]),
               wait=[(SE, SE.n)], inc=[(VE, 1)])
        p.emit("vector", lambda E_: E_.scalar_tensor_tensor(
            stats[:, 8:9], stats[:, 2:3], -1.0, stats[:, 7:8], op0=ALU.mult, op1=ALU.mult),
            wait=[(VE, VE.n)], inc=[(VE, 1)])
        mve = VE.n
        p.emit("scalar", lambda E_, i=i: E_.activation(
            out=lnf[:, :], in_=x_tm[:, i, :], func=AF.Identity,
            bias=stats[:, 8:9], scale=stats[:, 7:8]),
            wait=[(VE, mve)], inc=[(SE, 1)])
        if out16 is not None:
            p.emit("scalar", lambda E_, i=i, o=out16: E_.activation(
                out=o, in_=x_tm[:, i, :], func=AF.Identity,
                bias=stats[:, 8:9], scale=stats[:, 7:8]),
                inc=[(SE, 1)])
        return SE.n

    def transpose_to_fm(i, src16, dst_fm, mready_se):
        for g in range(4):
            mve_pre = VE.n
            for q in range(4):
                fc = 4 * g + q
                src_sl = src16[:, 128 * fc:128 * (fc + 1)]
                p.emit("tensor", lambda E_, s=src_sl, q=q: E_.transpose(
                    ps_s16[:, 1024 * q:1024 * q + 128], s, idt[:, :]),
                    wait=[(SE, mready_se), (VE, mve_pre)], inc=[(MM, 1)])
            mmm = MM.n
            for q in range(4):
                fc = 4 * g + q
                dst_sl = dst_fm[:, fc, 128 * i:128 * (i + 1)]
                p.emit("vector", lambda E_, d=dst_sl, q=q: E_.tensor_copy(
                    d, ps_s16[:, 1024 * q:1024 * q + 128]),
                    wait=[(MM, mmm)], inc=[(VE, 1)])
        return VE.n

    win_all = view(GC, 0, 16, 768)
    mwv = 0
    for di in range(16):
        mwv = p.dma("gpsimd", win_all[:, di, :], winT[128 * di:128 * (di + 1), :], GP,
                    fence=(di == 0))
    p.dma_slow("gpsimd", binsb[:, 0:6], bin_d.rearrange("(a p) -> p a", p=128), GP)
    mwv = GP.n
    for di in range(2):
        p.dma("gpsimd", wopb[:, di, :], woutT_d[128 * di:128 * (di + 1), :], GP,
              fence=True)
    mwopA = GP.n

    lnfm = view(GA, 0, 16, TOK)
    ln16 = view(GB, 0, 2048)
    for i in range(NT):
        mse = layernorm_tile(i, out16=ln16, mxw=[mx_t[i]])
        transpose_to_fm(i, ln16, lnfm, mse)
    mve_ln = VE.n
    for fc in range(16):
        for t in range(NTH):
            p.dma("sync", ag1c_t[t][128 * fc:128 * (fc + 1), :],
                  lnfm[:, fc, 512 * t:512 * (t + 1)], SY, wait=[(VE, mve_ln)])
    mstore1 = SY.n

    def emit_ag(kind, op, cin, cout, wait):
        p.emit("gpsimd", lambda E_, a=cin, b=cout, kind=kind, op=op: E_.collective_compute(
            kind, op, replica_groups=[list(range(NCORES))], ins=[a], outs=[b]),
            wait=wait, inc=[(CC, 1)])
        return CC.n

    m_ag1 = [emit_ag("AllGather", ALU.bypass, ag1c_t[t][:, :], ag1o_t[t][:, :],
                     [(SY, mstore1)] + ([(GP, GP.n)] if t == 0 else []))
             for t in range(NTH)]
    res_w1, res_w2 = {"m": 0}, {"m": 0}
    wq_units = (wquant_units(w1o_d, FSH, E, w1pc, res_w1) +
                wquant_units(w2o_d, ESH, FF, w2pc, res_w2))

    lnf16 = lnf[:, :].bitcast(F16)
    rhs_b = view(GC, 12288, 2, 4, 512)

    def proj_stream(ago, row_base, n_di, w_fn, do_list, evac, m_w, m_cc, banks=None,
                    bank_war=None):
        if banks is None:
            banks = [ps_s[:, 512 * kk:512 * kk + 512] for kk in range(len(do_list))]
        if not isinstance(m_cc, (list, tuple)):
            m_cc = [m_cc] * NTH
        nb = (n_di + 3) // 4
        batches = [(b, th, bt) for th in range(NTH) for b in range(NCORES)
                   for bt in range(nb)]
        buf_mm = [MM.n, MM.n]
        mld_of = {}
        first = True

        def emit_load(i):
            b, th, bt = batches[i]
            di0 = 4 * bt
            ndi = min(4, n_di - di0)
            buf = i % 2
            sem = SY if buf == 0 else SY2
            src_sl = ago[th][row_base(b) + 128 * di0: row_base(b) + 128 * (di0 + ndi),
                            0:512].rearrange("(a p) c -> p a c", p=128)
            w = [(CC, m_cc[th]), (MM, buf_mm[buf])]
            if i == 1:
                w.append(mld_of[0])
            mld = p.dma("sync", rhs_b[:, buf, 0:ndi, :], src_sl, sem,
                        wait=w, fence=(i == 1))
            return (sem, mld)

        def emit_mms(i):
            b, th, bt = batches[i]
            di0 = 4 * bt
            ndi = min(4, n_di - di0)
            buf = i % 2
            sem, mld = mld_of[i]
            for di in range(di0, di0 + ndi):
                for kk, do in enumerate(do_list):
                    w_sl = w_fn(di, do)
                    waits = [(sem, mld), (GP, m_w)]
                    if di == 0:
                        waits += ([(SE, SE.n), (VE, VE.n)] if bank_war is None
                                  else list(bank_war))
                    p.emit("tensor", lambda E_, w=w_sl, r=rhs_b[:, buf, di - di0, :],
                           o=banks[kk], st_=(di == 0), sp=(di == n_di - 1): E_.matmul(
                        o, w, r, start=st_, stop=sp),
                        wait=waits, inc=[(MM, 1)])
            buf_mm[buf] = MM.n
            if bt == nb - 1:
                mmm = MM.n
                for kk, do in enumerate(do_list):
                    evac(kk, do, b, th, mmm)

        mld_of[0] = (SY, p.dma("sync", rhs_b[:, 0, 0:min(4, n_di), :],
                               ago[0][row_base(0): row_base(0) + 128 * min(4, n_di),
                                      0:512].rearrange("(a p) c -> p a c", p=128), SY,
                               wait=[(CC, m_cc[0]), (MM, MM.n)], fence=True))
        for i in range(len(batches)):
            if i + 1 < len(batches):
                mld_of[i + 1] = emit_load(i + 1)
            emit_mms(i)

    wstr_b2 = view(GB, 12288, 2, 4, 512)

    def proj_local(src_fm, n_di, w_dram, col0, do_list, evac, mln):
        ncol = 128 * len(do_list)
        nb = (n_di + 3) // 4
        batches = [(th, bt) for th in range(NTH) for bt in range(nb)]
        buf_mm = [MM.n, MM.n]
        mld_of = {}

        def emit_load(i):
            th, bt = batches[i]
            di0 = 4 * bt
            ndi = min(4, n_di - di0)
            buf = i % 2
            sem = GP if buf == 0 else GP2
            wsl = w_dram[128 * di0:128 * (di0 + ndi), col0:col0 + ncol].rearrange(
                "(a p) c -> p a c", p=128)
            w = [(MM, buf_mm[buf])]
            if i == 0:
                w.append((VE, mln))
            elif i == 1:
                w.append(mld_of[0])
            mld = p.dma("gpsimd", wstr_b2[:, buf, 0:ndi, 0:ncol], wsl, sem,
                        wait=w, fence=(i <= 1))
            return (sem, mld)

        def emit_mms(i):
            th, bt = batches[i]
            di0 = 4 * bt
            ndi = min(4, n_di - di0)
            buf = i % 2
            sem, mld = mld_of[i]
            for di in range(di0, di0 + ndi):
                for kk in range(len(do_list)):
                    waits = [(sem, mld), (VE, mln)]
                    if di == 0:
                        waits += [(SE, SE.n)]
                    rsl = src_fm[:, di, 512 * th:512 * (th + 1)]
                    p.emit("tensor", lambda E_, w=wstr_b2[:, buf, di - di0, 0:ncol],
                           kk=kk, r=rsl, st_=(di == 0), sp=(di == n_di - 1): E_.matmul(
                        ps_s[:, 512 * kk:512 * kk + 512],
                        w[:, 128 * kk:128 * (kk + 1)], r, start=st_, stop=sp),
                        wait=waits, inc=[(MM, 1)])
            buf_mm[buf] = MM.n
            if bt == nb - 1:
                mmm = MM.n
                for kk, do in enumerate(do_list):
                    evac(kk, do, 0, th, mmm)

        mld_of[0] = emit_load(0)
        for i in range(len(batches)):
            if i + 1 < len(batches):
                mld_of[i + 1] = emit_load(i + 1)
            emit_mms(i)

    esc = lnf16[:, 0:512]
    esc2 = lnf16[:, 512:1024]
    pe_s = lnf16[:, 1024:1536]
    pesw_s = lnf16[:, 1536:2048]

    def norm_update(src16, slot, mready, rows=128, pb0=0):
        sq = esc2 if (rows == 128 and pb0 == 0) else esc2[pb0:pb0 + rows, :]
        s_ = src16
        p.emit("vector", lambda E_, s=s_, q=sq: E_.tensor_mul(q, s, s),
               wait=[mready, (VE, VE.n)], inc=[(VE, 1)])
        mve = VE.n
        oc = onec[:, :] if (rows == 128 and pb0 == 0) else onec[pb0:pb0 + rows, :]
        p.emit("tensor", lambda E_, q=sq, oc=oc: E_.matmul(
            bc_b[0:1, :], oc, q, start=True, stop=True),
            wait=[(VE, mve)], inc=[(MM, 1)])
        p.emit("vector", lambda E_: E_.tensor_reduce(
            sm1[:, 30:31], bc_b[0:1, :], axis=AX.X, op=ALU.max),
            wait=[(MM, MM.n)], inc=[(VE, 1)])
        p.emit("vector", lambda E_, sl=slot: E_.tensor_tensor(
            sm1[:, sl:sl + 1], sm1[:, sl:sl + 1], sm1[:, 30:31], op=ALU.max),
            wait=[(VE, VE.n)], inc=[(VE, 1)])
        return VE.n

    def center_norms(kview, slot, rows=128, extra_wait=()):
        sq16 = view(GC, 0, 2048)
        p.emit("vector", lambda E_, kv=kview: E_.reduce_sum(
            stats[:, 20:21], kv, axis=AX.X),
            wait=[(SE, SE.n), (VE, VE.n), (MM, MM.n)] + list(extra_wait), inc=[(VE, 1)])
        p.emit("vector", lambda E_: E_.tensor_scalar_mul(
            stats[:, 21:22], stats[:, 20:21], 1.0 / S_total),
            wait=[(VE, VE.n)], inc=[(VE, 1)])
        p.emit("vector", lambda E_, kv=kview: E_.tensor_single_scalar(
            kv, kv, stats[:, 21:22], op=ALU.subtract),
            wait=[(VE, VE.n)], inc=[(VE, 1)])
        mcent = VE.n
        oc = onec[:, :] if rows == 128 else onec[0:rows, :]
        for c2 in range(S_total // 2048):
            ksl = kview[:, 2048 * c2:2048 * (c2 + 1)]
            p.emit("vector", lambda E_, a=ksl: E_.tensor_mul(sq16, a, a),
                   wait=[(VE, max(mcent, VE.n)), (MM, MM.n)], inc=[(VE, 1)])
            mve = VE.n
            for q4 in range(4):
                if rows == 128:
                    sq_sl = sq16[:, 512 * q4:512 * (q4 + 1)]
                else:
                    sq_sl = sq16[0:rows, 512 * q4:512 * (q4 + 1)]
                p.emit("tensor", lambda E_, q=sq_sl, oc=oc: E_.matmul(
                    sum_b[0:1, :], oc, q, start=True, stop=True),
                    wait=[(VE, max(mve, VE.n))], inc=[(MM, 1)])
                p.emit("vector", lambda E_: E_.tensor_reduce(
                    sm1[:, 30:31], sum_b[0:1, :], axis=AX.X, op=ALU.max),
                    wait=[(MM, MM.n)], inc=[(VE, 1)])
                p.emit("vector", lambda E_, sl_=slot: E_.tensor_tensor(
                    sm1[:, sl_:sl_ + 1], sm1[:, sl_:sl_ + 1], sm1[:, 30:31], op=ALU.max),
                    wait=[(VE, VE.n)], inc=[(VE, 1)])
        return VE.n

    k_fm = view(GA, 0, 2, S_total)
    v_tm2 = view(GB, 0, NKT, 256)

    def evac_a1(kk, do, b, th, mmm):
        base = b * TOK + 512 * th
        pssl = [ps_s[:, 512 * q:512 * q + 512] for q in range(4)] + [pv_b, sum_b]
        pssl = pssl[kk]
        if do < 2:
            msy_pre = SY.n
            p.emit("scalar", lambda E_, ps_=pssl, bi=binsb[:, do:do + 1]: E_.activation(
                out=esc, in_=ps_, func=AF.Identity, bias=bi),
                wait=[(MM, mmm), (SY, msy_pre), (VE, VE.n)], inc=[(SE, 1)])
            mse = SE.n
            p.dma("sync", qa_dr[128 * do:128 * (do + 1), base:base + 512], esc, SY,
                  wait=[(SE, mse)])
            norm_update(esc, do, (SE, mse))
        elif do < 4:
            h = do - 2
            dst = k_fm[:, h, base:base + 512]
            p.emit("scalar", lambda E_, ps_=pssl, d=dst, bi=binsb[:, do:do + 1]: E_.activation(
                out=d, in_=ps_, func=AF.Identity, bias=bi),
                wait=[(MM, mmm), (SY, mstore1)], inc=[(SE, 1)])
        else:
            h = do - 4
            p.emit("scalar", lambda E_, ps_=pssl, bi=binsb[:, do:do + 1]: E_.activation(
                out=esc2, in_=ps_, func=AF.Identity, bias=bi),
                wait=[(MM, mmm), (VE, VE.n), (SY, SY.n)], inc=[(SE, 1)])
            mse = SE.n
            mve_pre = VE.n
            esc4 = esc2.rearrange("p (a c) -> p a c", a=4, c=128)
            for q in range(4):
                p.emit("tensor", lambda E_, s_=esc4[:, q, :], q=q: E_.transpose(
                    tx16[:, 128 * q:128 * (q + 1)], s_, idt[:, :]),
                    wait=[(SE, mse), (VE, mve_pre)], inc=[(MM, 1)])
            mt = MM.n
            for q in range(4):
                kt = (base + 128 * q) // 128
                dst = v_tm2[:, kt, 128 * h:128 * (h + 1)]
                p.emit("vector", lambda E_, d=dst, q=q: E_.tensor_copy(
                    d, tx16[:, 128 * q:128 * (q + 1)]),
                    wait=[(MM, mt)], inc=[(VE, 1)])

    proj_stream(ag1o_t, lambda b: b * E, 16,
                lambda di, do: win_all[:, di, 128 * do:128 * (do + 1)],
                [0, 1, 2, 3, 4, 5], evac_a1, mwv, m_ag1,
                banks=[ps_s[:, 0:512], ps_s[:, 512:1024], ps_s[:, 1024:1536],
                       ps_s[:, 1536:2048], pv_b, sum_b])

    center_norms(k_fm[:, 0, :], 2)
    center_norms(k_fm[:, 1, :], 3)

    def make_bias(col, slots_q, slots_k, scale):
        p.emit("vector", lambda E_, a=slots_q[0]: E_.tensor_copy(sm1[:, 28:29], sm1[:, a:a + 1]),
               wait=[(VE, VE.n)], inc=[(VE, 1)])
        for a in slots_q[1:]:
            p.emit("vector", lambda E_, a=a: E_.tensor_add(sm1[:, 28:29], sm1[:, 28:29], sm1[:, a:a + 1]),
                   wait=[(VE, VE.n)], inc=[(VE, 1)])
        p.emit("vector", lambda E_, a=slots_k[0]: E_.tensor_copy(sm1[:, 29:30], sm1[:, a:a + 1]),
               wait=[(VE, VE.n)], inc=[(VE, 1)])
        for a in slots_k[1:]:
            p.emit("vector", lambda E_, a=a: E_.tensor_add(sm1[:, 29:30], sm1[:, 29:30], sm1[:, a:a + 1]),
                   wait=[(VE, VE.n)], inc=[(VE, 1)])
        p.emit("vector", lambda E_: E_.tensor_tensor(sm1[:, 28:29], sm1[:, 28:29], sm1[:, 29:30], op=ALU.mult),
               wait=[(VE, VE.n)], inc=[(VE, 1)])
        p.emit("scalar", lambda E_: E_.activation(sm1[:, 28:29], sm1[:, 28:29], AF.Sqrt),
               wait=[(VE, VE.n)], inc=[(SE, 1)])
        p.emit("vector", lambda E_, sc=scale: E_.tensor_scalar(
            sm1[:, 28:29], sm1[:, 28:29], -sc * SAFE, EXPOFF, op0=ALU.mult, op1=ALU.add),
            wait=[(SE, SE.n)], inc=[(VE, 1)])
        mve = VE.n
        mmm_pre = MM.n
        p.emit("tensor", lambda E_: E_.matmul(bc_b[:, 0:1], onerf[:, :], sm1[:, 28:29],
                                              start=True, stop=True),
               wait=[(VE, mve)], inc=[(MM, 1)])
        p.emit("vector", lambda E_, col=col: E_.tensor_copy(bcol[:, col:col + 1], bc_b[:, 0:1]),
               wait=[(MM, MM.n)], inc=[(VE, 1)])
        return VE.n

    make_bias(0, [0], [2], SC_A)
    m_biasA = make_bias(1, [1], [3], SC_A)
    if DEBUG:
        dbg_sm1 = nc.dram_tensor("dbg_sm1", [1, 32], F32, kind="ExternalOutput").ap()
        dbg_bcol = nc.dram_tensor("dbg_bcol", [128, 4], F32, kind="ExternalOutput").ap()
        dbg_kfm = nc.dram_tensor("dbg_kfm", [128, 512], F16, kind="ExternalOutput").ap()
        dbg_q = nc.dram_tensor("dbg_q", [128, 512], F16, kind="ExternalOutput").ap()
        p.dma("sync", dbg_sm1[:, :], sm1[:, :], SY, wait=[(VE, m_biasA)])
        p.dma("sync", dbg_bcol[:, 0:2], bcol[:, 0:2], SY)
        p.dma("sync", dbg_kfm[:, :], k_fm[:, 0, 0:512], SY)
        p.dma("gpsimd", dbg_q[:, :], qa_dr[0:128, 0:512], GP, wait=[(SY, SY.n)])

    BF16 = mybir.dt.bfloat16
    pt = view(GC, 0, 2, 2, 512).bitcast(BF16)
    ptp = [view(GC, 0, 2, 1024).bitcast(BF16)[:, 0, :],
           view(GC, 0, 2, 1024).bitcast(BF16)[:, 1, :]]
    qblk = view(GC, 2048, 2, 2, 512)
    qpeb = view(GC, 12288, 2, 512)
    av_sc = view(GC, 13312, 512)
    kpe_t = view(GC, 4096, S_total)

    dbg_sums = []
    atnB_dr_marker = [None]

    def attention(q_dr_, atn_dr_, scale, bias_cols, v_src, with_pe, m_ready, bg=()):
        bg_state = [0]

        def bg_step(frac):
            tgt = min(len(bg), int(len(bg) * min(1.0, (frac * 4.0 / 3.0) ** 2) + 1e-9))
            while bg_state[0] < tgt:
                bg[bg_state[0]]()
                bg_state[0] += 1
        qpar_mm = [0, 0]
        sexp = [0, 0]
        vmask = [[], []]
        msump = [0, 0]
        m_ones = [MM.n]
        mld_of = {}
        epi = [None]

        def flush_epi():
            if epi[0] is not None:
                epi[0]()
                epi[0] = None

        def load_q(Qb):
            par = Qb % 2
            sem = SY if par == 0 else SY2
            w = [(MM, qpar_mm[par])] + (list(m_ready) if Qb == 0 else [])
            p.dma("sync", qblk[:, par, 0, :], q_dr_[0:128, 512 * Qb:512 * (Qb + 1)],
                  sem, wait=w, fence=(Qb == 0))
            p.dma("sync", qblk[:, par, 1, :], q_dr_[128:256, 512 * Qb:512 * (Qb + 1)], sem)
            if with_pe:
                p.dma("sync", qpeb[:, par, :], q_dr_[256:384, 512 * Qb:512 * (Qb + 1)], sem)
            return (sem, sem.n)

        mld_of[0] = load_q(0)
        for Qb in range(NQB):
            par_q = Qb % 2
            qsem, mld = mld_of[Qb]
            if Qb + 1 < NQB:
                mld_of[Qb + 1] = load_q(Qb + 1)
            nk = 4 * (Qb + 1)
            for h in range(2):
                ngrp = (nk + 1) // 2
                pend = [None]

                def flush_pv():
                    if pend[0] is not None:
                        pend[0]()
                        pend[0] = None

                for g in range(ngrp):
                    pg = g % 2
                    gt = [j for j in (2 * g, 2 * g + 1) if j < nk]
                    ng = len(gt)
                    for j in gt:
                        pssl = ps_s[:, 1024 * pg + 512 * (j - 2 * g):
                                    1024 * pg + 512 * (j - 2 * g) + 512]
                        ksl = k_fm[:, h, 128 * j:128 * (j + 1)]
                        p.emit("tensor", lambda E_, o=pssl, k=ksl, r=qblk[:, par_q, h, :],
                               sp=(not with_pe): E_.matmul(o, k, r, start=True, stop=sp),
                            wait=[(qsem, mld), (SE, sexp[pg])], inc=[(MM, 1)])
                    if with_pe:
                        for j in gt:
                            pssl = ps_s[:, 1024 * pg + 512 * (j - 2 * g):
                                        1024 * pg + 512 * (j - 2 * g) + 512]
                            kpesl = kpe_t[64 * h:64 * (h + 1), 128 * j:128 * (j + 1)]
                            qpesl = qpeb[64 * h:64 * (h + 1), par_q, :]
                            p.emit("tensor", lambda E_, o=pssl, k=kpesl, r=qpesl: E_.matmul(
                                o, k, r, start=False, stop=True),
                                inc=[(MM, 1)])
                    flush_epi()
                    flush_pv()
                    mgrp = MM.n
                    p.emit("scalar", lambda E_, ng=ng, pg=pg,
                           bc=bcol[:, bias_cols[h]:bias_cols[h] + 1], sc=scale: E_.activation(
                        out=ptp[pg][:, 0:512 * ng], in_=ps_s[:, 1024 * pg:1024 * pg + 512 * ng],
                        func=AF.Exp, bias=bc, scale=sc),
                        wait=[(MM, max(mgrp, msump[pg]))] + list(vmask[pg]), inc=[(SE, 1)])
                    sexp[pg] = SE.n
                    for j in gt:
                        if j >= 4 * Qb:
                            psl = pt[:, pg, j - 2 * g, :]
                            p.emit("vector", lambda E_, a=psl, m=lt[:, j - 4 * Qb, :]: E_.tensor_mul(a, a, m),
                                   wait=[(SE, sexp[pg])], inc=[(VE, 1)])
                    mmask = VE.n
                    for j in gt:
                        psl = pt[:, pg, j - 2 * g, :]
                        if j % 2 == 0:
                            if j == 0:
                                p.emit("vector", lambda E_, a=psl: E_.tensor_copy(pacc[:, :], a),
                                       wait=[(SE, sexp[pg]), (VE, mmask), (MM, m_ones[0])],
                                       inc=[(VE, 1)])
                            else:
                                p.emit("vector", lambda E_, a=psl: E_.tensor_tensor(
                                    pacc[:, :], pacc[:, :], a, op=ALU.add),
                                    wait=[(SE, sexp[pg]), (VE, VE.n)], inc=[(VE, 1)])
                        else:
                            if j == 1:
                                p.emit("gpsimd", lambda E_, a=psl: E_.tensor_copy(pacc2[:, :], a),
                                       wait=[(SE, sexp[pg]), (VE, mmask), (MM, m_ones[0]),
                                             (PL, PL.n)],
                                       inc=[(PL, 1)])
                            else:
                                p.emit("gpsimd", lambda E_, a=psl: E_.tensor_tensor(
                                    pacc2[:, :], pacc2[:, :], a, op=ALU.add),
                                    wait=[(SE, sexp[pg]), (VE, mmask), (PL, PL.n)],
                                    inc=[(PL, 1)])
                    vmask[pg] = [(VE, VE.n), (PL, PL.n)]

                    def mk_pv(gt=gt, pg=pg, g=g, h=h, nk=nk,
                              sexp_v=sexp[pg], mmask_v=mmask):
                        for j in gt:
                            psl = pt[:, pg, j - 2 * g, :]
                            vsl = v_src[:, j, 128 * h:128 * (h + 1)]
                            p.emit("tensor", lambda E_, v=vsl, a=psl, st_=(j == 0),
                                   sp=(j == nk - 1): E_.matmul(
                                pv_b[:, :], v, a, start=st_, stop=sp),
                                wait=[(SE, sexp_v), (VE, mmask_v)], inc=[(MM, 1)])
                        msump[pg] = MM.n
                    pend[0] = mk_pv
                flush_pv()
                qpar_mm[par_q] = MM.n

                def mk_epi(h=h, Qb=Qb,
                           ve_m=max(vmask[0][0][1], vmask[1][0][1]),
                           pl_m=max(vmask[0][1][1], vmask[1][1][1])):
                    p.emit("tensor", lambda E_: E_.matmul(sum_b[0:1, :], onecf[:, :], pacc[:, :],
                                                          start=True, stop=False),
                           wait=[(VE, ve_m)], inc=[(MM, 1)])
                    p.emit("tensor", lambda E_: E_.matmul(sum_b[0:1, :], onecf[:, :], pacc2[:, :],
                                                          start=False, stop=True),
                           wait=[(PL, pl_m)], inc=[(MM, 1)])
                    m_ones[0] = MM.n
                    mpv = MM.n
                    p.emit("vector", lambda E_: E_.reciprocal(rsum[:, :], sum_b[0:1, :]),
                           wait=[(MM, mpv), (VE, VE.n)], inc=[(VE, 1)])
                    mve = VE.n
                    p.emit("tensor", lambda E_: E_.matmul(bc_b[:, :], onerf[:, :], rsum[:, :],
                                                          start=True, stop=True),
                           wait=[(VE, mve)], inc=[(MM, 1)])
                    p.emit("vector", lambda E_: E_.tensor_copy(rbc[:, :], bc_b[:, :]),
                           wait=[(MM, MM.n)], inc=[(VE, 1)])
                    msy_pre = SY.n
                    p.emit("vector", lambda E_: E_.tensor_tensor(av_sc, pv_b[:, :], rbc[:, :], op=ALU.mult),
                           wait=[(VE, VE.n), (SY, msy_pre)], inc=[(VE, 1)])
                    mve = VE.n
                    p.dma("sync", atn_dr_[128 * h:128 * (h + 1), 512 * Qb:512 * (Qb + 1)],
                          av_sc, SY, wait=[(VE, mve)])
                epi[0] = mk_epi
                bg_step((Qb * 2 + h + 1) / (NQB * 2))
        flush_epi()
        bg_step(1.0)
        return SY.n

    m_atnA = attention(qa_dr, atnA_dr, SC_A, [0, 1], v_tm2, False,
                       [(VE, m_biasA), (MM, MM.n)], bg=wq_units)

    emit_ag("AllGather", ALU.bypass, w1pc[:, :], w1po[:, :], [(SC, res_w1["m"])])
    m_agw = emit_ag("AllGather", ALU.bypass, w2pc[:, :], w2po[:, :], [(SC, res_w2["m"])])

    def out_proj(atn_dr_, w_dram, pXh, m_atn, mw=None):
        wop = wopb
        if mw is None:
            for di in range(2):
                mw = p.dma("gpsimd", wop[:, di, :], w_dram[128 * di:128 * (di + 1), :],
                           GP, wait=[(MM, MM.n)], fence=True)
        atile = view(GC, 4096, 2, 512)
        mstore = [0, 0]
        if TOK >= 1024:
            ts_list = [t_ for t_ in range(NTT) if (512 * t_ % TOK) < TOK // 2] + \
                      [t_ for t_ in range(NTT) if (512 * t_ % TOK) >= TOK // 2]
        else:
            ts_list = list(range(NTT))
        for idx, ts in enumerate(ts_list):
            for di in range(2):
                mmm_pre = MM.n
                src_sl = atn_dr_[128 * di:128 * (di + 1), 512 * ts:512 * (ts + 1)]
                mld = p.dma("sync", atile[:, di, :], src_sl, SY,
                            wait=[(SY, m_atn), (MM, mmm_pre)],
                            fence=(di == 0 and (idx == 0 or idx == NTT // 2)))
            mld = SY.n
            for q in range(4):
                for di in range(2):
                    for dsl in range(4):
                        a_sl = atile[:, di, 128 * q:128 * (q + 1)]
                        w_sl = wop[:, di, 512 * dsl:512 * (dsl + 1)]
                        p.emit("tensor", lambda E_, a=a_sl, w=w_sl, dsl=dsl,
                               st_=(di == 0), sp=(di == 1): E_.matmul(
                            ps_s[:, 512 * dsl:512 * dsl + 512], a, w, start=st_, stop=sp),
                            wait=[(SY, mld), (GP, mw)] + ([(SE, SE.n)] if di == 0 else []),
                            inc=[(MM, 1)])
                mmm = MM.n
                r0 = 512 * ts + 128 * q
                TOK2 = TOK // 2
                hch = (r0 % TOK) // TOK2
                row = (r0 // TOK) * TOK2 + (r0 % TOK2)
                stg4 = lnf16[:, 2048 * (q % 2):2048 * (q % 2) + 2048]
                qeng, qsem = (("sync", SY) if q % 2 == 0 else ("scalar", SC))
                msy_pre = qsem.n
                for dsl in range(4):
                    p.emit("scalar", lambda E_, dsl=dsl, o=stg4[:, 512 * dsl:512 * (dsl + 1)]:
                           E_.activation(out=o, in_=ps_s[:, 512 * dsl:512 * dsl + 512],
                                         func=AF.Identity),
                           wait=[(MM, mmm), (qsem, msy_pre)], inc=[(SE, 1)])
                p.dma(qeng, pXh[hch][row:row + 128, :], stg4, qsem,
                      wait=[(SE, SE.n)], fence=True)
            if idx == NTT // 2 - 1:
                mstore[0] = [(SY, SY.n), (SC, SC.n)]
            mstore[1] = [(SY, SY.n), (SC, SC.n)]
        if TOK < 1024:
            mstore[0] = mstore[1]
        return mstore

    mp2 = out_proj(atnA_dr, woutT_d, p2h, m_atnA, mw=mwopA)
    m_rs2 = [emit_ag("ReduceScatter", ALU.add, p2h[t][:, :], rs2h[t][:, :],
                     mp2[t]) for t in range(2)]

    def mk_residual(rsh, m_rs, with_bias):
        NT2 = NT // 2
        def pre(th):
            halves = [th] if NTH == 2 else [0, 1]
            for h in halves:
                for j in range(NT2):
                    i = h * NT2 + j
                    rt = view(GB, 0, NT, 2048)[:, i, :]
                    mve_pre = VE.n
                    mld = p.dma("sync", rt, rsh[h][128 * j:128 * (j + 1), :], SY,
                                wait=[(CC, m_rs[h]), (VE, mve_pre)], fence=True)
                    if with_bias:
                        p.emit("vector", lambda E_, r=rt: E_.tensor_add(r, r, boutbc[:, :]),
                               wait=[(SY, mld), (VE, m_boutbc)], inc=[(VE, 1)])
                    p.emit("vector", lambda E_, i=i, r=rt: E_.tensor_add(
                        x_tm[:, i, :], x_tm[:, i, :], r),
                        wait=[(SY, mld), (VE, VE.n)], inc=[(VE, 1)])
        return pre

    res2 = mk_residual(rs2h, m_rs2, True)
    if DEBUG:
        dump_x("dbg_x1")

    def ffn_phase(m_w, pre_th=None, ffn_dbg=False):
        xq = view(GA, 0, 16, 512)
        wbuf = view(GA, 8192, 2, 1024)
        q16 = view(GA, 10752, 512)
        stg = view(GA, 12288, 8, 512)

        def hq_tile(hc):
            if hc < 32:
                return view(GB, 512 * hc, 512)
            return view(GC, 512 * (hc - 32), 512)

        for th in range(NTH):
            if pre_th is not None:
                pre_th(th)
            for q in range(4):
                i = 4 * th + q
                mse = layernorm_tile(i)
                p.emit("vector", lambda E_: E_.reduce_max(
                    stats[:, 9:10], lnf[:, :], axis=AX.X, apply_absolute_value=True),
                    wait=[(SE, mse), (VE, VE.n)], inc=[(VE, 1)])
                p.emit("vector", lambda E_: E_.tensor_single_scalar(
                    stats[:, 10:11], stats[:, 9:10], 1e-5, op=ALU.max),
                    wait=[(VE, VE.n)], inc=[(VE, 1)])
                p.emit("vector", lambda E_: E_.reciprocal(stats[:, 11:12], stats[:, 10:11]),
                       wait=[(VE, VE.n)], inc=[(VE, 1)])
                p.emit("vector", lambda E_: E_.tensor_scalar_mul(stats[:, 12:13], stats[:, 11:12], 127.0),
                       wait=[(VE, VE.n)], inc=[(VE, 1)])
                p.emit("vector", lambda E_: E_.tensor_scalar_mul(stats[:, 13:14], stats[:, 10:11], 1.0 / 127.0),
                       wait=[(VE, VE.n)], inc=[(VE, 1)])
                mve_s = VE.n
                for sl in range(4):
                    a = lnf[:, 512 * sl:512 * (sl + 1)]
                    p.emit("vector", lambda E_, a=a: E_.tensor_scalar(
                        hmax[:, :], a, stats[:, 12:13], RMAGIC, op0=ALU.mult, op1=ALU.add),
                        wait=[(VE, max(mve_s, VE.n)), (SE, SE.n)], inc=[(VE, 1)])
                    p.emit("vector", lambda E_: E_.tensor_scalar_add(
                        fsc[:, :], hmax[:, :], -RMAGIC),
                        wait=[(VE, VE.n)], inc=[(VE, 1)])
                    mve_q = VE.n
                    mmm_pre = MM.n
                    p.emit("scalar", lambda E_: E_.activation(
                        out=q16, in_=fsc[:, :], func=AF.Identity, scale=stats[:, 13:14]),
                        wait=[(VE, mve_q), (MM, mmm_pre)], inc=[(SE, 1)])
                    mse2 = SE.n
                    mve_pre = VE.n
                    q164 = q16.rearrange("p (a c) -> p a c", a=4, c=128)
                    for qq in range(4):
                        fc = 4 * sl + qq
                        p.emit("tensor", lambda E_, s=q164[:, qq, :], qq=qq: E_.transpose(
                            tx16[:, 128 * qq:128 * (qq + 1)], s, idt[:, :]),
                            wait=[(SE, mse2), (VE, mve_pre)], inc=[(MM, 1)])
                    mt = MM.n
                    for qq in range(4):
                        fc = 4 * sl + qq
                        dst = xq[:, fc, 128 * q:128 * (q + 1)]
                        p.emit("vector", lambda E_, d=dst, qq=qq: E_.tensor_copy(
                            d, tx16[:, 128 * qq:128 * (qq + 1)]),
                            wait=[(MM, mt)], inc=[(VE, 1)])
            mxq = VE.n
            if DEBUG and ffn_dbg and th == 0:
                dbg_xq = nc.dram_tensor("dbg_xq", [128, 512], F16, kind="ExternalOutput").ap()
                p.dma("sync", dbg_xq[:, :], xq[:, 0, :], SY, wait=[(VE, mxq)])
                p.emit("vector", lambda E_: E_.engine_nop(), wait=[(SY, SY.n)], inc=[(VE, 1)])
            p.emit("vector", lambda E_: E_.memset(hmax[:, :], 0.0),
                   wait=[(VE, VE.n)], inc=[(VE, 1)])
            f1banks = [ps_s[:, 512 * q:512 * (q + 1)] for q in range(4)] + \
                      [ps_p[:, 512 * q:512 * (q + 1)] for q in range(4)]
            buf_mm1 = [MM.n, MM.n]
            for hgp in range(8):
                for di in range(16):
                    buf = di % 2
                    qeng, qsem = (("sync", SY) if (di % 2 == 0) else ("scalar", SC))
                    wsl = w1po[1024 * hgp:1024 * (hgp + 1), 128 * di:128 * (di + 1)]
                    mld = p.dmaT(qeng, wbuf[:, buf, :], wsl, qsem,
                                 wait=[(CC, m_w), (MM, buf_mm1[buf])], fence=True)
                    for hs in range(8):
                        w_sl = wbuf[:, buf, 128 * hs:128 * (hs + 1)]
                        waits = [(qsem, mld), (VE, mxq)]
                        if di == 0:
                            waits += [(SE, SE.n), (VE, VE.n)]
                        p.emit("tensor", lambda E_, w=w_sl, r=xq[:, di, :], o=f1banks[hs],
                               st_=(di == 0), sp=(di == 15): E_.matmul(
                            o, w, r, start=st_, stop=sp),
                            wait=waits, inc=[(MM, 1)])
                    buf_mm1[buf] = MM.n
                mmm = MM.n
                for hs in range(8):
                    hc = 8 * hgp + hs
                    dst = hq_tile(hc)
                    p.emit("scalar", lambda E_, d=dst, o=f1banks[hs]: E_.activation(
                        out=d, in_=o, func=AF.Gelu_apprx_tanh),
                        wait=[(MM, mmm)], inc=[(SE, 1)])
                    p.emit("vector", lambda E_, d=dst: E_.tensor_tensor(
                        rbc[:, :], d, d, op=ALU.mult),
                        wait=[(SE, SE.n), (VE, VE.n)], inc=[(VE, 1)])
                    p.emit("vector", lambda E_: E_.tensor_tensor(
                        hmax[:, :], hmax[:, :], rbc[:, :], op=ALU.max),
                        wait=[(VE, VE.n)], inc=[(VE, 1)])
            mhm = VE.n
            if DEBUG and ffn_dbg and th == 0:
                dbg_h = nc.dram_tensor("dbg_h", [128, 512], F16, kind="ExternalOutput").ap()
                p.dma("sync", dbg_h[:, :], hq_tile(0), SY, wait=[(VE, mhm)])
                dbg_hmax = nc.dram_tensor("dbg_hmax", [128, 512], F32, kind="ExternalOutput").ap()
                p.dma("sync", dbg_hmax[:, :], hmax[:, :], SY)
                p.emit("vector", lambda E_: E_.engine_nop(), wait=[(SY, SY.n)], inc=[(VE, 1)])
            for qq in range(4):
                p.emit("tensor", lambda E_, s=hmax[:, 128 * qq:128 * (qq + 1)], qq=qq: E_.transpose(
                    tx_b[:, 128 * qq:128 * (qq + 1)], s, idtf[:, :]),
                    wait=[(VE, mhm), (VE, VE.n)], inc=[(MM, 1)])
                p.emit("vector", lambda E_, qq=qq: E_.reduce_max(
                    stats[:, 14:15], tx_b[:, 128 * qq:128 * (qq + 1)], axis=AX.X),
                    wait=[(MM, MM.n), (VE, VE.n)], inc=[(VE, 1)])
                p.emit("scalar", lambda E_: E_.activation(
                    stats[:, 14:15], stats[:, 14:15], AF.Sqrt),
                    wait=[(VE, VE.n), (SE, SE.n)], inc=[(SE, 1)])
                p.emit("vector", lambda E_: E_.tensor_single_scalar(
                    stats[:, 15:16], stats[:, 14:15], 1e-5, op=ALU.max),
                    wait=[(VE, VE.n), (SE, SE.n)], inc=[(VE, 1)])
                p.emit("vector", lambda E_: E_.reciprocal(stats[:, 16:17], stats[:, 15:16]),
                       wait=[(VE, VE.n)], inc=[(VE, 1)])
                p.emit("vector", lambda E_: E_.tensor_scalar_mul(
                    stats[:, 17:18], stats[:, 16:17], 127.0),
                    wait=[(VE, VE.n)], inc=[(VE, 1)])
                p.emit("vector", lambda E_: E_.tensor_scalar_mul(
                    stats[:, 18:19], stats[:, 15:16], 1.0 / 127.0),
                    wait=[(VE, VE.n)], inc=[(VE, 1)])
                mcol = VE.n
                for (col, dstf) in ((17, s2f), (18, rs2f)):
                    p.emit("tensor", lambda E_, c=stats[:, col:col + 1]: E_.transpose(
                        tx_b[0:1, 384:512], c, idtf[:, :]),
                        wait=[(VE, max(mcol, VE.n))], inc=[(MM, 1)])
                    p.emit("vector", lambda E_: E_.tensor_copy(sr8[0:1, :], tx_b[0:1, 384:512]),
                           wait=[(MM, MM.n)], inc=[(VE, 1)])
                    mrow = VE.n
                    p.emit("tensor", lambda E_: E_.matmul(
                        bc_b[:, 0:128], onerf[:, :], sr8[0:1, :], start=True, stop=True),
                        wait=[(VE, mrow)], inc=[(MM, 1)])
                    p.emit("vector", lambda E_, d=dstf[:, 128 * qq:128 * (qq + 1)]: E_.tensor_copy(
                        d, bc_b[:, 0:128]),
                        wait=[(MM, MM.n)], inc=[(VE, 1)])
            mvq = VE.n
            if DEBUG and ffn_dbg and th == 0:
                dbg_s2 = nc.dram_tensor("dbg_s2", [128, 512], F32, kind="ExternalOutput").ap()
                dbg_rs2 = nc.dram_tensor("dbg_rs2", [128, 512], F32, kind="ExternalOutput").ap()
                p.dma("sync", dbg_s2[:, :], s2f[:, :], SY, wait=[(VE, mvq)])
                p.dma("sync", dbg_rs2[:, :], rs2f[:, :], SY)
                p.emit("vector", lambda E_: E_.engine_nop(), wait=[(SY, SY.n)], inc=[(VE, 1)])
            mhq_t = {}
            for hc in range(64):
                t_ = hq_tile(hc)
                p.emit("vector", lambda E_, t=t_: E_.tensor_tensor(rbc[:, :], t, s2f[:, :], op=ALU.mult),
                       wait=[(VE, max(mvq, VE.n))], inc=[(VE, 1)])
                p.emit("vector", lambda E_: E_.tensor_scalar_add(hmax[:, :], rbc[:, :], RMAGIC),
                       wait=[(VE, VE.n)], inc=[(VE, 1)])
                p.emit("vector", lambda E_: E_.tensor_scalar_add(rbc[:, :], hmax[:, :], -RMAGIC),
                       wait=[(VE, VE.n)], inc=[(VE, 1)])
                p.emit("vector", lambda E_, t=t_: E_.tensor_tensor(t, rbc[:, :], rs2f[:, :], op=ALU.mult),
                       wait=[(VE, VE.n)], inc=[(VE, 1)])
                mhq_t[hc] = VE.n
            mhq = VE.n
            if DEBUG and ffn_dbg and th == 0:
                dbg_hq = nc.dram_tensor("dbg_hq", [128, 512], F16, kind="ExternalOutput").ap()
                p.dma("sync", dbg_hq[:, :], hq_tile(0), SY, wait=[(VE, mhq)])
                msy = SY.n
                p.emit("vector", lambda E_: E_.engine_nop(), wait=[(SY, msy)], inc=[(VE, 1)])
            buf_mm2 = [MM.n, MM.n]
            m_stg = 0
            for dgp in range(2):
                for hi in range(64):
                    buf = hi % 2
                    qeng, qsem = (("sync", SY) if (hi % 2 == 0) else ("scalar", SC))
                    wsl = w2po[1024 * dgp:1024 * (dgp + 1), 128 * hi:128 * (hi + 1)]
                    mld = p.dmaT(qeng, wbuf[:, buf, :], wsl, qsem,
                                 wait=[(CC, m_w), (MM, buf_mm2[buf])], fence=True)
                    for d8 in range(8):
                        w_sl = wbuf[:, buf, 128 * d8:128 * (d8 + 1)]
                        waits = [(qsem, mld), (VE, max(mhq_t[hi], m_stg))]
                        if hi == 0:
                            waits += [(SE, SE.n)]
                        p.emit("tensor", lambda E_, w=w_sl, r=hq_tile(hi), o=f1banks[d8],
                               st_=(hi == 0), sp=(hi == 63): E_.matmul(
                            o, w, r, start=st_, stop=sp),
                            wait=waits, inc=[(MM, 1)])
                    buf_mm2[buf] = MM.n
                mmm = MM.n
                for d8 in range(8):
                    p.emit("scalar", lambda E_, d=stg[:, d8, :], o=f1banks[d8]: E_.activation(
                        out=d, in_=o, func=AF.Identity),
                        wait=[(MM, mmm), (VE, VE.n)], inc=[(SE, 1)])
                mse = SE.n
                for d8 in range(8):
                    do_c = 8 * dgp + d8
                    mve_pre = VE.n
                    st4 = stg[:, d8, :].rearrange("p (a c) -> p a c", a=4, c=128)
                    for tq in range(4):
                        p.emit("tensor", lambda E_, s_=st4[:, tq, :], tq=tq: E_.transpose(
                            tx16[:, 128 * tq:128 * (tq + 1)], s_, idt[:, :]),
                            wait=[(SE, mse), (VE, mve_pre)], inc=[(MM, 1)])
                    mt = MM.n
                    for tq in range(4):
                        i = 4 * th + tq
                        dst = x_tm[:, i, 128 * do_c:128 * (do_c + 1)]
                        p.emit("vector", lambda E_, d=dst, tq=tq: E_.tensor_tensor(
                            d, d, tx16[:, 128 * tq:128 * (tq + 1)], op=ALU.add),
                            wait=[(MM, mt)], inc=[(VE, 1)])
                m_stg = VE.n
        return VE.n

    ffn_phase(m_agw, pre_th=res2, ffn_dbg=True)
    if DEBUG:
        dump_x("dbg_x2")
        dbg_w1p = nc.dram_tensor("dbg_w1p", [128, 2048], F16, kind="ExternalOutput").ap()
        p.dma("sync", dbg_w1p[:, :], w1po[0:128, :], SY, wait=[(CC, m_agw)])

    ln3fm = view(GA, 0, 16, TOK)
    ln16b = view(GB, 4096, 2048)
    for i in range(NT):
        mse = layernorm_tile(i, out16=ln16b)
        transpose_to_fm(i, ln16b, ln3fm, mse)
    mln3 = VE.n

    p.emit("gpsimd", lambda E_, o=gainsb[:, 0:12],
           i=qaln_d.rearrange("(a p) -> p a", p=128): E_.dma_start(
        out=o, in_=i, allow_slow_non_contiguous=True),
        wait=[(GP, GP.n)], inc=[(GP, 16)])
    p.emit("gpsimd", lambda E_, o=gainsb[:, 12:16],
           i=kvln_d.rearrange("(a p) -> p a", p=128): E_.dma_start(
        out=o, in_=i, allow_slow_non_contiguous=True),
        wait=[(GP, GP.n)], inc=[(GP, 16)])
    mgain = GP.n

    qa_s = view(GB, 0, 12, TOK)
    def mk_evac_store(dst_fm):
        def ev(kk, do, b, th, mmm):
            dsl = dst_fm[:, do, 512 * th:512 * (th + 1)]
            p.emit("scalar", lambda E_, d=dsl, kk=kk: E_.activation(
                out=d, in_=ps_s[:, 512 * kk:512 * kk + 512], func=AF.Identity),
                wait=[(MM, mmm)], inc=[(SE, 1)])
        return ev

    proj_local(ln3fm, 16, wqa_d, 0, [0, 1, 2, 3], mk_evac_store(qa_s), mln3)
    proj_local(ln3fm, 16, wqa_d, 512, [4, 5, 6, 7], mk_evac_store(qa_s), mln3)
    proj_local(ln3fm, 16, wqa_d, 1024, [8, 9, 10, 11], mk_evac_store(qa_s), mln3)
    mqa = SE.n

    ckv_s = view(GC, 4096, 4, TOK)
    kpe_l = view(GC, 8192, TOK)
    kpesw_l = view(GC, 9216, TOK)

    def ev_kva(kk, do, b, th, mmm):
        if do < 4:
            dsl = ckv_s[:, do, 512 * th:512 * (th + 1)]
            p.emit("scalar", lambda E_, d=dsl, kk=kk: E_.activation(
                out=d, in_=ps_s[:, 512 * kk:512 * kk + 512], func=AF.Identity),
                wait=[(MM, mmm)], inc=[(SE, 1)])
        else:
            dsl = (kpe_l if do == 4 else kpesw_l)[0:64, 512 * th:512 * (th + 1)]
            p.emit("scalar", lambda E_, d=dsl, kk=kk: E_.activation(
                out=d, in_=ps_s[0:64, 512 * kk:512 * kk + 512], func=AF.Identity),
                wait=[(MM, mmm)], inc=[(SE, 1)])

    proj_local(ln3fm, 16, wkva_d, 0, [0, 1, 2, 3], ev_kva, mln3)
    for th in range(NTH):
        for bt in range(4):
            buf = bt % 2
            mmm_pre = MM.n
            di0 = 4 * bt
            wsl = wkva_d[128 * di0:128 * (di0 + 4), 512:640].rearrange(
                "(a p) c -> p a c", p=128)
            dbuf = wstr_b2[:, buf, 0:4, 0:128]
            mld = p.dma("gpsimd", dbuf, wsl, GP, wait=[(MM, mmm_pre)],
                        fence=(th == 0 and bt == 0))
            for di in range(di0, di0 + 4):
                for kk in range(2):
                    waits = [(GP, mld), (VE, mln3)]
                    if di == 0:
                        waits += [(SE, SE.n)]
                    rsl = ln3fm[:, di, 512 * th:512 * (th + 1)]
                    p.emit("tensor", lambda E_, w=dbuf[:, di - di0, :], kk=kk, r=rsl,
                           st_=(di == 0), sp=(di == 15): E_.matmul(
                        ps_s[0:64, 512 * kk:512 * kk + 512],
                        w[:, 64 * kk:64 * (kk + 1)], r, start=st_, stop=sp),
                        wait=waits, inc=[(MM, 1)])
        mmm = MM.n
        for kk in range(2):
            ev_kva(kk, 4 + kk, 0, th, mmm)
    mkva = SE.n

    def rms_apply(src_fm, nchunk, gain_off, dst_rows, eps, mproj, denom):
        for th in range(NTH):
            mve_pre = VE.n
            for ch in range(nchunk):
                ssl = src_fm[:, ch, 512 * th:512 * (th + 1)]
                p.emit("vector", lambda E_, s=ssl: E_.tensor_mul(esc2, s, s),
                       wait=[(SE, mproj), (MM, MM.n),
                             (VE, mve_pre if ch == 0 else VE.n)], inc=[(VE, 1)])
                mve = VE.n
                p.emit("tensor", lambda E_, st_=(ch == 0), sp=(ch == nchunk - 1): E_.matmul(
                    sum_b[0:1, :], onec[:, :], esc2, start=st_, stop=sp),
                    wait=[(VE, mve)], inc=[(MM, 1)])
            mmm = MM.n
            p.emit("vector", lambda E_, d=denom: E_.tensor_scalar(
                rsum[:, :], sum_b[0:1, :], 1.0 / d, eps, op0=ALU.mult, op1=ALU.add),
                wait=[(MM, mmm), (VE, VE.n)], inc=[(VE, 1)])
            p.emit("scalar", lambda E_: E_.activation(rsum[:, :], rsum[:, :], AF.Sqrt),
                   wait=[(VE, VE.n)], inc=[(SE, 1)])
            p.emit("vector", lambda E_: E_.reciprocal(rsum[:, :], rsum[:, :]),
                   wait=[(SE, SE.n)], inc=[(VE, 1)])
            mve = VE.n
            p.emit("tensor", lambda E_: E_.matmul(bc_b[:, :], onerf[:, :], rsum[:, :],
                                                  start=True, stop=True),
                   wait=[(VE, mve)], inc=[(MM, 1)])
            p.emit("vector", lambda E_: E_.tensor_copy(rbc[:, :], bc_b[:, :]),
                   wait=[(MM, MM.n)], inc=[(VE, 1)])
            mrbc = VE.n
            for ch in range(nchunk):
                ssl = src_fm[:, ch, 512 * th:512 * (th + 1)]
                msy_pre = SY.n
                p.emit("vector", lambda E_, s=ssl, g=gainsb[:, gain_off + ch:gain_off + ch + 1]:
                       E_.scalar_tensor_tensor(esc, s, g, rbc[:, :], op0=ALU.mult, op1=ALU.mult),
                       wait=[(VE, max(mrbc, VE.n)), (SY, msy_pre), (GP, mgain)], inc=[(VE, 1)])
                p.dma("sync", ag3c_t[th][dst_rows + 128 * ch:dst_rows + 128 * (ch + 1),
                                         0:512], esc, SY,
                      wait=[(VE, VE.n)])
        return SY.n

    rms_apply(qa_s, 12, 0, 0, 1e-6, mqa, QL)
    mag3a = rms_apply(ckv_s, 4, 12, QL, 1e-6, mkva, KL)
    mag3_t = []

    cosk_sb = view(GB, 12288, TOK)
    sink_sb = view(GB, 13312, TOK)
    p.dma("gpsimd", cosk_sb[0:64, :], cosk_d[:, :], GP,
          wait=[(MM, MM.n), (VE, VE.n), (SY, SY.n)])
    p.dma("gpsimd", sink_sb[0:64, :], sink_d[:, :], GP)
    mck = GP.n
    for th in range(NTH):
        sl = slice(512 * th, 512 * (th + 1))
        a_ = kpe_l[0:64, sl]
        b_ = kpesw_l[0:64, sl]
        c_ = cosk_sb[0:64, sl]
        d_ = sink_sb[0:64, sl]
        e1 = esc[0:64, :]
        e2 = esc2[0:64, :]
        p.emit("vector", lambda E_, a=a_, c=c_, e=e1: E_.tensor_tensor(e, a, c, op=ALU.mult),
               wait=[(SE, mkva), (GP, mck), (VE, VE.n), (SY, SY.n)], inc=[(VE, 1)])
        p.emit("vector", lambda E_, b=b_, d=d_, e=e2: E_.tensor_tensor(e, b, d, op=ALU.mult),
               wait=[(VE, VE.n)], inc=[(VE, 1)])
        p.emit("vector", lambda E_, e=e1, f=e2: E_.tensor_tensor(e, e, f, op=ALU.add),
               wait=[(VE, VE.n)], inc=[(VE, 1)])
        mve = VE.n
        mag3_t.append(p.dma("sync", ag3c_t[th][QL + KL:QL + KL + 64, 0:512], e1, SY,
                            wait=[(VE, mve)], fence=True))
    m_ag3 = [emit_ag("AllGather", ALU.bypass, ag3c_t[t][:, :], ag3o_t[t][:, :],
                     [(SY, mag3_t[t])])
             for t in range(NTH)]

    wqb_sb = view(GB, 0, 12, 512)
    _kvb16a = s2f[:, :].bitcast(F16)
    _kvb16b = rs2f[:, :].bitcast(F16)

    def wkvb_tile(di):
        g = _kvb16a if di < 2 else _kvb16b
        return g[:, 512 * (di % 2):512 * (di % 2 + 1)]
    mwb = 0
    for di in range(12):
        mwb = p.dma("gpsimd", wqb_sb[:, di, :], wqb_d[128 * di:128 * (di + 1), :], GP,
                    wait=[(VE, VE.n)], fence=(di == 0))
    for di in range(4):
        mwb = p.dma("gpsimd", wkvb_tile(di), wkvb_d[128 * di:128 * (di + 1), :], GP,
                    wait=[(VE, VE.n)])

    cq_sb = view(GB, 12288, 512)
    sq_sb = view(GB, 12800, 512)

    def ev_qb(kk, do, b, th, mmm):
        base = b * TOK + 512 * th
        pssl = ps_s[:, 512 * kk:512 * kk + 512]
        if do < 2:
            msy_pre = SY.n
            p.emit("scalar", lambda E_, ps_=pssl: E_.activation(
                out=esc, in_=ps_, func=AF.Identity),
                wait=[(MM, mmm), (SY, msy_pre), (VE, VE.n)], inc=[(SE, 1)])
            mse = SE.n
            p.dma("sync", qb_dr[128 * do:128 * (do + 1), base:base + 512], esc, SY,
                  wait=[(SE, mse)])
            norm_update(esc, 4 + do, (SE, mse))
        elif do == 2:
            p.emit("scalar", lambda E_, ps_=pssl: E_.activation(
                out=pe_s, in_=ps_, func=AF.Identity),
                wait=[(MM, mmm), (VE, VE.n)], inc=[(SE, 1)])
        else:
            p.emit("scalar", lambda E_, ps_=pssl: E_.activation(
                out=pesw_s, in_=ps_, func=AF.Identity),
                wait=[(MM, mmm)], inc=[(SE, 1)])
            mse = SE.n
            mgq = p.dma("gpsimd", cq_sb, cosq_d[:, base:base + 512], GP,
                        wait=[(VE, VE.n), (MM, MM.n)])
            mgq = p.dma("gpsimd", sq_sb, sinq_d[:, base:base + 512], GP)
            p.emit("vector", lambda E_: E_.tensor_tensor(esc, pe_s, cq_sb, op=ALU.mult),
                   wait=[(SE, mse), (GP, mgq), (VE, VE.n), (SY, SY.n)], inc=[(VE, 1)])
            p.emit("vector", lambda E_: E_.tensor_tensor(esc2, pesw_s, sq_sb, op=ALU.mult),
                   wait=[(VE, VE.n)], inc=[(VE, 1)])
            msy_pre = SY.n
            p.emit("vector", lambda E_: E_.tensor_tensor(esc, esc, esc2, op=ALU.add),
                   wait=[(VE, VE.n), (SY, msy_pre)], inc=[(VE, 1)])
            mve = VE.n
            p.dma("sync", qb_dr[256:384, base:base + 512], esc, SY, wait=[(VE, mve)])
            norm_update(esc[0:64, :], 6, (VE, mve), rows=64, pb0=0)
            norm_update(esc[64:128, :], 7, (VE, VE.n), rows=64, pb0=64)
        return

    proj_stream(ag3o_t, lambda b: b * AG3R, 12,
                lambda di, do: wqb_sb[:, di, 128 * do:128 * (do + 1)],
                [0, 1, 2, 3], ev_qb, mwb, m_ag3)

    def ev_kvb(kk, do, b, th, mmm):
        base = b * TOK + 512 * th
        pssl = ps_s[:, 512 * kk:512 * kk + 512]
        if do < 2:
            h = do
            dst = k_fm[:, h, base:base + 512]
            p.emit("scalar", lambda E_, ps_=pssl, d=dst: E_.activation(
                out=d, in_=ps_, func=AF.Identity),
                wait=[(MM, mmm)], inc=[(SE, 1)])
        else:
            h = do - 2
            pssl = ps_s[:, 512 * kk:512 * kk + 512]
            base = b * TOK + 512 * th
            p.emit("scalar", lambda E_, ps_=pssl: E_.activation(
                out=esc2, in_=ps_, func=AF.Identity),
                wait=[(MM, mmm), (VE, VE.n), (SY, SY.n)], inc=[(SE, 1)])
            mse = SE.n
            mve_pre = VE.n
            esc4 = esc2.rearrange("p (a c) -> p a c", a=4, c=128)
            for q in range(4):
                p.emit("tensor", lambda E_, s_=esc4[:, q, :], q=q: E_.transpose(
                    tx16[:, 128 * q:128 * (q + 1)], s_, idt[:, :]),
                    wait=[(SE, mse), (VE, mve_pre)], inc=[(MM, 1)])
            mt = MM.n
            for q in range(4):
                kt = (base + 128 * q) // 128
                dst = v_tm2[:, kt, 128 * h:128 * (h + 1)]
                p.emit("vector", lambda E_, d=dst, q=q: E_.tensor_copy(
                    d, tx16[:, 128 * q:128 * (q + 1)]),
                    wait=[(MM, mt)], inc=[(VE, 1)])

    proj_stream(ag3o_t, lambda b: b * AG3R + QL, 4,
                lambda di, do: wkvb_tile(di)[:, 128 * do:128 * (do + 1)],
                [0, 1, 2, 3], ev_kvb, mwb, m_ag3)

    mkpe = 0
    first = True
    for b in range(NCORES):
        for t in range(NTH):
            src = ag3o_t[t][b * AG3R + QL + KL: b * AG3R + QL + KL + 64, :]
            cols = slice(b * TOK + 512 * t, b * TOK + 512 * (t + 1))
            w = [(MM, MM.n), (VE, VE.n), (SY, SY.n), (CC, m_ag3[-1])] if first else []
            first = False
            mkpe = p.dma("sync", kpe_t[0:64, cols], src, SY, wait=w)
            mkpe = p.dma("sync", kpe_t[64:128, cols], src, SY)

    center_norms(k_fm[:, 0, :], 8)
    center_norms(k_fm[:, 1, :], 9)
    center_norms(kpe_t[:, :], 10, rows=64, extra_wait=[(SY, mkpe)])

    make_bias(2, [4, 6], [8, 10], SC_B)
    m_biasB = make_bias(3, [5, 7], [9, 10], SC_B)
    if DEBUG:
        dbg2_sm1 = nc.dram_tensor("dbg2_sm1", [1, 32], F32, kind="ExternalOutput").ap()
        dbg2_bcol = nc.dram_tensor("dbg2_bcol", [128, 4], F32, kind="ExternalOutput").ap()
        dbg_kb = nc.dram_tensor("dbg_kb", [128, 512], F16, kind="ExternalOutput").ap()
        dbg_qb = nc.dram_tensor("dbg_qb", [128, 512], F16, kind="ExternalOutput").ap()
        dbg_kpe = nc.dram_tensor("dbg_kpe", [128, 512], F16, kind="ExternalOutput").ap()
        dbg_qpe = nc.dram_tensor("dbg_qpe", [128, 512], F16, kind="ExternalOutput").ap()
        p.dma("sync", dbg2_sm1[:, :], sm1[:, :], SY, wait=[(VE, m_biasB)])
        p.dma("sync", dbg2_bcol[:, :], bcol[:, :], SY)
        p.dma("sync", dbg_kb[:, :], k_fm[:, 0, 0:512], SY)
        p.dma("sync", dbg_kpe[:, :], kpe_t[:, 0:512], SY)
        p.dma("gpsimd", dbg_qb[:, :], qb_dr[0:128, 0:512], GP, wait=[(SY, SY.n)])
        p.dma("gpsimd", dbg_qpe[:, :], qb_dr[256:384, 0:512], GP)

    atnB_dr_marker[0] = atnB_dr
    m_atnB = attention(qb_dr, atnB_dr, SC_B, [2, 3], v_tm2, True,
                       [(VE, m_biasB), (MM, MM.n)])

    mp4 = out_proj(atnB_dr, woutb_d, p4h, m_atnB)
    m_rs4 = [emit_ag("ReduceScatter", ALU.add, p4h[t][:, :], rs4h[t][:, :],
                     mp4[t]) for t in range(2)]
    res4 = mk_residual(rs4h, m_rs4, False)
    if DEBUG:
        dump_x("dbg_x3")

    ffn_phase(m_agw, pre_th=res4)

    mfin = 0
    for i in range(NT):
        mve_pre = VE.n
        mfin = p.dma("sync", y_out[128 * i:128 * (i + 1), :], x_tm[:, i, :], SY,
                     wait=[(VE, mve_pre)])
    p.emit("gpsimd", lambda E_: E_.engine_nop(), wait=[(SY, mfin)])

    p.finalize()
    return nc


_NC_CACHE = {}


def kernel(**inputs):
    S_total = int(np.asarray(inputs["L0_attn_a_hidden_states"]).shape[1])
    if S_total not in _NC_CACHE:
        _NC_CACHE[S_total] = build(S_total)
    nc = _NC_CACHE[S_total]
    maps = host_inputs(inputs, S_total)
    res = run_bass_kernel_spmd(nc, maps, core_ids=list(range(NCORES)))
    out = np.concatenate([res.results[c]["y"] for c in range(NCORES)], axis=0)
    return out.reshape(1, S_total, E).astype(np.float32)


# version 19
# speedup vs baseline: 1.0361x; 1.0361x over previous
import math
import numpy as np

import concourse.bass as bass
import concourse.mybir as mybir
from concourse.bass_utils import run_bass_kernel_spmd

DEBUG = False

F32 = mybir.dt.float32
F16 = mybir.dt.float16
I32 = mybir.dt.int32
AF = mybir.ActivationFunctionType
ALU = mybir.AluOpType
AX = mybir.AxisListType

NCORES = 8
E = 2048
HDA = 128
QL, KL = 1536, 512
DN, DR, DV = 128, 64, 128
DQ = DN + DR
FF = 8192
SC_A = 1.0 / math.sqrt(HDA)
SC_B = 1.0 / math.sqrt(DQ)
ROPE_BASE = 10000.0
SAFE = 1.02
RMAGIC = 12582912.0
EXPOFF = 60.0

ENGINES = ("tensor", "vector", "scalar", "gpsimd", "sync")


class S:
    __slots__ = ("h", "n", "name")

    def __init__(self, h, name):
        self.h = h
        self.n = 0
        self.name = name


class PB:
    def __init__(self, nc):
        self.nc = nc
        self.q = {e: [] for e in ENGINES}
        self._waited = {e: {} for e in ENGINES}

    def sem(self, name):
        return S(self.nc.alloc_semaphore(name), name)

    def emit(self, eng, fn, wait=(), inc=()):
        w = []
        for s, v in wait:
            if v > self._waited[eng].get(id(s), -1):
                self._waited[eng][id(s)] = v
                w.append((s.h, v))
        ii = []
        for s, k in inc:
            s.n += k
            ii.append((s.h, k))
        self.q[eng].append((fn, w, ii))

    def dma(self, eng, out, in_, sem, wait=(), fence=False):
        if fence:
            wait = [(sem, sem.n)] + list(wait)
        self.emit(eng, lambda E_, o=out, i=in_: E_.dma_start(out=o, in_=i),
                  wait=wait, inc=[(sem, 16)])
        return sem.n

    def dma_slow(self, eng, out, in_, sem, wait=()):
        self.emit(eng, lambda E_, o=out, i=in_: E_.dma_start(
            out=o, in_=i, allow_slow_non_contiguous=True),
            wait=wait, inc=[(sem, 16)])
        return sem.n

    def dmaT(self, eng, out, in_, sem, wait=(), fence=False):
        if fence:
            wait = [(sem, sem.n)] + list(wait)
        self.emit(eng, lambda E_, o=out, i=in_: E_.dma_start_transpose(o, i),
                  wait=wait, inc=[(sem, 16)])
        return sem.n

    def finalize(self):
        with self.nc.Block() as block:
            for eng in ENGINES:
                stream = self.q[eng]
                if not stream:
                    continue
                getattr(block, eng)(_mk_runner(stream))


def _mk_runner(stream):
    def run(Eng):
        for fn, w, ii in stream:
            for h, v in w:
                Eng.wait_ge(h, v)
            ins = fn(Eng)
            for h, k in ii:
                ins.then_inc(h, k)
    return run


def _rope_tables(S_total):
    inv_freq = 1.0 / ROPE_BASE ** (np.arange(0, DR, 2, dtype=np.float64) / DR)
    t = np.arange(S_total, dtype=np.float64)
    fr = np.outer(inv_freq, t)
    return np.cos(fr).astype(np.float32), np.sin(fr).astype(np.float32)


def host_inputs(inp, S_total):
    TOK = S_total // NCORES
    x = np.asarray(inp["L0_attn_a_hidden_states"]).reshape(S_total, E)
    W_in = np.asarray(inp["L0_attn_a_W_in"])
    b_in = np.asarray(inp["L0_attn_a_b_in"])
    W_out = np.asarray(inp["L0_attn_a_W_out"])
    b_out = np.asarray(inp["L0_attn_a_b_out"])
    wqa = np.asarray(inp["L0_attn_b_w_q_a"])
    qaln = np.asarray(inp["L0_attn_b_q_a_ln"])
    wqb = np.asarray(inp["L0_attn_b_w_q_b"])
    wkva = np.asarray(inp["L0_attn_b_w_kv_a"])
    kvln = np.asarray(inp["L0_attn_b_kv_a_ln"])
    wkvb = np.asarray(inp["L0_attn_b_w_kv_b"])
    woutb = np.asarray(inp["L0_attn_b_w_out"])
    w1 = np.asarray(inp["L0_ffn_w1"])
    w2 = np.asarray(inp["L0_ffn_w2"])

    WinT = np.ascontiguousarray(W_in.T)
    WoutT = np.ascontiguousarray(W_out.T)
    perm = np.concatenate([np.arange(0, DR, 2), np.arange(1, DR, 2)])
    c32, s32 = _rope_tables(S_total)
    cosq = np.ascontiguousarray(np.concatenate([c32, c32, c32, c32], axis=0))
    sinq = np.ascontiguousarray(np.concatenate([-s32, s32, -s32, s32], axis=0))

    maps = []
    for c in range(NCORES):
        h0, h1 = 2 * c, 2 * c + 1
        cols = np.concatenate([base + np.arange(HDA * h0, HDA * (h1 + 1))
                               for base in (0, E, 2 * E)])
        winT_own = np.ascontiguousarray(WinT[:, cols])
        bin_own = np.ascontiguousarray(b_in[cols])

        def qb_cols(h):
            base = DQ * h
            return (base + np.arange(DN), base + DN + perm[:32], base + DN + perm[32:])

        n0, l0, hh0 = qb_cols(h0)
        n1, l1, hh1 = qb_cols(h1)
        pe = np.concatenate([l0, hh0, l1, hh1])
        pesw = np.concatenate([hh0, l0, hh1, l1])
        wqb_own = np.ascontiguousarray(wqb[:, np.concatenate([n0, n1, pe, pesw])])

        def kvb_cols(h):
            base = (DN + DV) * h
            return base + np.arange(DN), base + DN + np.arange(DV)

        kn0, v0 = kvb_cols(h0)
        kn1, v1 = kvb_cols(h1)
        wkvb_own = np.ascontiguousarray(wkvb[:, np.concatenate([kn0, kn1, v0, v1])])
        pe_cols = KL + perm
        pesw_cols = KL + np.concatenate([perm[32:], perm[:32]])
        wkva_own = np.ascontiguousarray(
            np.concatenate([wkva[:, :KL], wkva[:, pe_cols], wkva[:, pesw_cols]], axis=1))
        ts = slice(TOK * c, TOK * (c + 1))
        maps.append({
            "x_own": np.ascontiguousarray(x[ts]),
            "winT_own": winT_own,
            "bin_own": bin_own,
            "woutT_own": np.ascontiguousarray(WoutT[256 * c:256 * (c + 1)]),
            "b_out": np.ascontiguousarray(b_out),
            "wqa": np.ascontiguousarray(wqa),
            "qaln": np.ascontiguousarray(qaln),
            "wqb_own": wqb_own,
            "wkva_own": wkva_own,
            "kvln": np.ascontiguousarray(kvln),
            "wkvb_own": wkvb_own,
            "woutb_own": np.ascontiguousarray(woutb[256 * c:256 * (c + 1)]),
            "w1_own": np.ascontiguousarray(w1[(FF // NCORES) * c:(FF // NCORES) * (c + 1)]),
            "w2_own": np.ascontiguousarray(w2[(E // NCORES) * c:(E // NCORES) * (c + 1)]),
            "cosq": cosq,
            "sinq": sinq,
            "cosk_own": np.ascontiguousarray(np.concatenate([c32[:, ts], c32[:, ts]], axis=0)),
            "sink_own": np.ascontiguousarray(np.concatenate([-s32[:, ts], s32[:, ts]], axis=0)),
        })
    return maps


def build(S_total):
    TOK = S_total // NCORES
    NT = TOK // 128
    NTH = TOK // 512
    NKT = S_total // 128
    NQB = S_total // 512
    NTT = S_total // 512
    FSH = FF // NCORES
    ESH = E // NCORES
    assert TOK % 512 == 0 and NKT * 256 <= 16384

    nc = bass.Bass("TRN2", target_bir_lowering=False, debug=False)

    din = lambda n, sh: nc.dram_tensor(n, sh, F32, kind="ExternalInput").ap()
    x_in = din("x_own", [TOK, E])
    winT = din("winT_own", [E, 768])
    bin_d = din("bin_own", [768])
    woutT_d = din("woutT_own", [256, E])
    bout_d = din("b_out", [E])
    wqa_d = din("wqa", [E, QL])
    qaln_d = din("qaln", [QL])
    wqb_d = din("wqb_own", [QL, 512])
    wkva_d = din("wkva_own", [E, KL + 2 * DR])
    kvln_d = din("kvln", [KL])
    wkvb_d = din("wkvb_own", [KL, 512])
    woutb_d = din("woutb_own", [256, E])
    w1o_d = din("w1_own", [FSH, E])
    w2o_d = din("w2_own", [ESH, FF])
    cosq_d = din("cosq", [128, S_total])
    sinq_d = din("sinq", [128, S_total])
    cosk_d = din("cosk_own", [64, TOK])
    sink_d = din("sink_own", [64, TOK])
    y_out = nc.dram_tensor("y", [TOK, E], F32, kind="ExternalOutput").ap()

    dint = lambda n, sh: nc.dram_tensor(n, sh, F16).ap()
    dsh = lambda n, sh: nc.dram_tensor(n, sh, F16, addr_space="Shared").ap()
    NTH_ = max(1, TOK // 512)
    ag1c_t = [dint(f"ag1c{t}", [E, 512]) for t in range(NTH_)]
    ag1o_t = [dsh(f"ag1o{t}", [NCORES * E, 512]) for t in range(NTH_)]
    AG3R = QL + KL + DR
    ag3c_t = [dint(f"ag3c{t}", [AG3R, 512]) for t in range(NTH_)]
    ag3o_t = [dsh(f"ag3o{t}", [NCORES * AG3R, 512]) for t in range(NTH_)]
    w1pc = dint("w1pc", [FSH, E])
    w1po = dsh("w1po", [FF, E])
    w2pc = dint("w2pc", [ESH, FF])
    w2po = dsh("w2po", [E, FF])
    qa_dr = dint("qa_dr", [256, S_total])
    qb_dr = dint("qb_dr", [384, S_total])
    atnA_dr = dint("atnA_dr", [256, S_total])
    atnB_dr = dint("atnB_dr", [256, S_total])
    p2h = [dint("p2a", [S_total // 2, E]), dint("p2b", [S_total // 2, E])]
    rs2h = [dint("rs2a", [TOK // 2, E]), dint("rs2b", [TOK // 2, E])]
    p4h = [dint("p4a", [S_total // 2, E]), dint("p4b", [S_total // 2, E])]
    rs4h = [dint("rs4a", [TOK // 2, E]), dint("rs4b", [TOK // 2, E])]

    LTm = np.zeros((512, 512), dtype=np.float16)
    for k in range(512):
        LTm[k, k:] = 1
    LT_d = nc.inline_tensor(LTm, name="LTm").ap()
    ID_d = nc.inline_tensor(np.eye(128, dtype=np.float16), name="IDm").ap()
    IDf_d = nc.inline_tensor(np.eye(128, dtype=np.float32), name="IDf").ap()

    p = PB(nc)
    SY = p.sem("SY")
    GP = p.sem("GP")
    MM = p.sem("MM")
    VE = p.sem("VE")
    SE = p.sem("SE")
    CC = p.sem("CC")
    SC = p.sem("SC")
    SY2 = p.sem("SY2")
    GP2 = p.sem("GP2")

    sb = lambda n, sh, dt: nc.alloc_sbuf_tensor(n, sh, dt)
    x_tm = sb("x_tm", [128, NT, E], F32)
    GA = sb("GA", [128, 16384], F16)
    GB = sb("GB", [128, 16384], F16)
    GC = sb("GC", [128, 16384], F16)
    lt = sb("lt", [128, 4, 512], F16)
    idt = sb("idt", [128, 128], F16)
    idtf = sb("idtf", [128, 128], F32)
    onec = sb("onec", [128, 1], F16)
    onecf = sb("onecf", [128, 1], F32)
    oner16 = sb("oner16", [1, 128], F16)
    onerf = sb("onerf", [1, 128], F32)
    pacc = sb("pacc", [128, 512], F32)
    stats = sb("stats", [128, 24], F32)
    sm1 = sb("sm1", [1, 32], F32)
    bcol = sb("bcol", [128, 4], F32)
    rsum = sb("rsum", [1, 512], F32)
    rbc = sb("rbc", [128, 512], F32)
    fsc = sb("fsc", [128, 512], F32)
    s2f = sb("s2f", [128, 512], F32)
    rs2f = sb("rs2f", [128, 512], F32)
    i32s = sb("i32s", [128, 512], I32)
    binsb = sb("binsb", [128, 6], F32)
    gainsb = sb("gainsb", [128, 16], F32)
    lnf = sb("lnf", [128, E], F32)
    hmax = sb("hmax", [128, 512], F32)
    sr8 = sb("sr8", [8, 128], F32)
    boutbc = sb("boutbc", [128, E], F16)
    wsc = sb("wsc", [128, 512], F16)
    wopb = sb("wopb", [128, 2, 2048], F16)

    ps_s = nc.alloc_psum_tensor("ps_s", [128, 2048], F32)
    ps_p = nc.alloc_psum_tensor("ps_p", [128, 2048], F32)
    pv_b = ps_p[:, 0:512]
    sum_b = ps_p[:, 512:1024]
    bc_b = ps_p[:, 1024:1536]
    tx_b = ps_p[:, 1536:2048]
    ps_s16 = ps_s[:, :].bitcast(F16)
    tx16 = ps_p[:, 1536:2048].bitcast(F16)

    def view(G, off, *shape):
        n = int(np.prod(shape))
        assert off + n <= 16384, (off, shape)
        v = G[:, off:off + n]
        if len(shape) == 1:
            return v
        names = " ".join(f"a{i}" for i in range(len(shape)))
        return v.rearrange(f"p ({names}) -> p {names}",
                           **{f"a{i}": s for i, s in enumerate(shape)})

    p.dma("sync", idt[:, :], ID_d[:, :], SY)
    p.dma("sync", idtf[:, :], IDf_d[:, :], SY)
    for j in range(4):
        p.dma("sync", lt[:, j, :], LT_d[128 * j:128 * (j + 1), :], SY)
    p.emit("vector", lambda E_: E_.memset(onec[:, :], 1.0), inc=[(VE, 1)])
    p.emit("vector", lambda E_: E_.memset(onecf[:, :], 1.0), inc=[(VE, 1)])
    p.emit("vector", lambda E_: E_.memset(oner16[:, :], 1.0), inc=[(VE, 1)])
    p.emit("vector", lambda E_: E_.memset(onerf[:, :], 1.0), inc=[(VE, 1)])
    p.emit("vector", lambda E_: E_.memset(sm1[:, :], 0.0), inc=[(VE, 1)])
    mcons = VE.n
    mx_t = []
    for i in range(NT):
        sem = SY if i % 2 == 0 else SY2
        mx_t.append((sem, p.dma("sync", x_tm[:, i, :], x_in[128 * i:128 * (i + 1), :],
                                sem, fence=True)))
    mx = [mx_t[-2], mx_t[-1]] if NT >= 2 else [mx_t[-1]]
    b16row = sb("b16row", [1, E], F16)
    p.dma("gpsimd", b16row[:, :], bout_d[None, :], GP)
    mb16 = GP.n
    for q4 in range(4):
        bsl = b16row[:, 512 * q4:512 * (q4 + 1)]
        p.emit("tensor", lambda E_, b=bsl: E_.matmul(
            bc_b[:, :], oner16[:, :], b, start=True, stop=True),
            wait=[(GP, mb16), (VE, max(mcons, VE.n)), (SY, SY.n)], inc=[(MM, 1)])
        dsl = boutbc[:, 512 * q4:512 * (q4 + 1)]
        p.emit("vector", lambda E_, d=dsl: E_.tensor_copy(d, bc_b[:, :]),
               wait=[(MM, MM.n)], inc=[(VE, 1)])
    m_boutbc = VE.n

    def dump_x(name):
        d = nc.dram_tensor(name, [TOK, E], F32, kind="ExternalOutput").ap()
        mve = VE.n
        for i in range(NT):
            p.dma("sync", d[128 * i:128 * (i + 1), :], x_tm[:, i, :], SY,
                  wait=[(VE, mve)])
        msy = SY.n
        p.emit("vector", lambda E_: E_.engine_nop(), wait=[(SY, msy)], inc=[(VE, 1)])

    def wquant_units(src, n_rows, width, dst, res):
        units = []
        for ch in range(n_rows // 128):
            def u_p1(ch=ch, g_=None):
                for g in ([g_] if g_ is not None else []):
                    src_sl = src[128 * ch:128 * (ch + 1), 2048 * g:2048 * (g + 1)]
                    mve_pre = VE.n
                    mld = p.dma("sync", lnf[:, :], src_sl, SC,
                                wait=[(VE, mve_pre), (SE, SE.n)] if ch == 0 and g == 0
                                else [(VE, mve_pre)],
                                fence=True)
                    p.emit("vector", lambda E_: E_.reduce_sum(
                        stats[:, 16:17], lnf[:, :], axis=AX.X, apply_absolute_value=True),
                        wait=[(SC, mld)], inc=[(VE, 1)])
                    if g == 0:
                        p.emit("vector", lambda E_: E_.tensor_copy(stats[:, 17:18], stats[:, 16:17]),
                               wait=[(VE, VE.n)], inc=[(VE, 1)])
                    else:
                        p.emit("vector", lambda E_: E_.tensor_add(
                            stats[:, 17:18], stats[:, 17:18], stats[:, 16:17]),
                            wait=[(VE, VE.n)], inc=[(VE, 1)])

            def u_sc(ch=ch):
                p.emit("vector", lambda E_, w=width: E_.tensor_scalar(
                    stats[:, 18:19], stats[:, 17:18], 1.0 / w, 1e-5, op0=ALU.mult, op1=ALU.max),
                    wait=[(VE, VE.n)], inc=[(VE, 1)])
                p.emit("vector", lambda E_: E_.reciprocal(stats[:, 19:20], stats[:, 18:19]),
                       wait=[(VE, VE.n)], inc=[(VE, 1)])

            def u_p2(ch=ch, g_=0):
                g = g_
                src_sl = src[128 * ch:128 * (ch + 1), 2048 * g:2048 * (g + 1)]
                mve_pre = VE.n
                mld = p.dma("sync", lnf[:, :], src_sl, SC, wait=[(VE, mve_pre)],
                            fence=True)
                for sl in range(4):
                    a = lnf[:, 512 * sl:512 * (sl + 1)]
                    mse_pre = SE.n
                    p.emit("vector", lambda E_, a=a: E_.tensor_scalar(
                        fsc[:, :], a, stats[:, 19:20], 1.4999999, op0=ALU.mult, op1=ALU.min),
                        wait=[(SC, mld), (VE, VE.n), (SE, mse_pre)],
                        inc=[(VE, 1)])
                    p.emit("vector", lambda E_: E_.tensor_scalar(
                        hmax[:, :], fsc[:, :], -1.4999999, RMAGIC, op0=ALU.max, op1=ALU.add),
                        wait=[(VE, VE.n)], inc=[(VE, 1)])
                    p.emit("vector", lambda E_: E_.tensor_scalar_add(
                        fsc[:, :], hmax[:, :], -RMAGIC),
                        wait=[(VE, VE.n)], inc=[(VE, 1)])
                    msc_pre = SC.n
                    p.emit("vector", lambda E_: E_.tensor_single_scalar(
                        wsc[:, :], fsc[:, :], stats[:, 18:19], op=ALU.mult),
                        wait=[(VE, VE.n), (SC, msc_pre)], inc=[(VE, 1)])
                    p.dma("sync",
                          dst[128 * ch:128 * (ch + 1),
                              2048 * g + 512 * sl:2048 * g + 512 * (sl + 1)],
                          wsc[:, :], SC, wait=[(VE, VE.n)])
                res["m"] = SC.n

            for g in range(width // 2048):
                units.append(lambda f=u_p1, ch=ch, g=g: f(ch, g))
            units.append(u_sc)
            for g in range(width // 2048):
                units.append(lambda f=u_p2, ch=ch, g=g: f(ch, g))
        return units

    def layernorm_tile(i, out16=None, mw=None, mxw=None):
        mxw = mx if mxw is None else mxw
        p.emit("vector", lambda E_, i=i: E_.reduce_sum(stats[:, 0:1], x_tm[:, i, :], axis=AX.X),
               wait=[(VE, mw if mw is not None else VE.n)] + list(mxw), inc=[(VE, 1)])
        scr = view(GB, 2048, 2048)
        p.emit("scalar", lambda E_, i=i, scr=scr: E_.activation(
            out=scr, in_=x_tm[:, i, :], func=AF.Square, accum_out=stats[:, 1:2]),
            wait=[(SE, SE.n), (VE, mw if mw is not None else VE.n)] + list(mxw), inc=[(SE, 1)])
        mse = SE.n
        p.emit("vector", lambda E_: E_.tensor_scalar_mul(stats[:, 2:3], stats[:, 0:1], 1.0 / E),
               wait=[(VE, VE.n), (SE, mse)], inc=[(VE, 1)])
        p.emit("vector", lambda E_: E_.tensor_scalar_mul(stats[:, 3:4], stats[:, 1:2], 1.0 / E),
               inc=[(VE, 1)])
        p.emit("vector", lambda E_: E_.tensor_tensor(stats[:, 4:5], stats[:, 2:3], stats[:, 2:3], op=ALU.mult),
               wait=[(VE, VE.n)], inc=[(VE, 1)])
        p.emit("vector", lambda E_: E_.scalar_tensor_tensor(
            stats[:, 5:6], stats[:, 3:4], 1e-5, stats[:, 4:5], op0=ALU.add, op1=ALU.subtract),
            wait=[(VE, VE.n)], inc=[(VE, 1)])
        p.emit("scalar", lambda E_: E_.activation(stats[:, 6:7], stats[:, 5:6], AF.Sqrt),
               wait=[(VE, VE.n)], inc=[(SE, 1)])
        p.emit("vector", lambda E_: E_.reciprocal(stats[:, 7:8], stats[:, 6:7]),
               wait=[(SE, SE.n)], inc=[(VE, 1)])
        p.emit("vector", lambda E_: E_.scalar_tensor_tensor(
            stats[:, 8:9], stats[:, 2:3], -1.0, stats[:, 7:8], op0=ALU.mult, op1=ALU.mult),
            wait=[(VE, VE.n)], inc=[(VE, 1)])
        mve = VE.n
        p.emit("scalar", lambda E_, i=i: E_.activation(
            out=lnf[:, :], in_=x_tm[:, i, :], func=AF.Identity,
            bias=stats[:, 8:9], scale=stats[:, 7:8]),
            wait=[(VE, mve)], inc=[(SE, 1)])
        if out16 is not None:
            p.emit("scalar", lambda E_, i=i, o=out16: E_.activation(
                out=o, in_=x_tm[:, i, :], func=AF.Identity,
                bias=stats[:, 8:9], scale=stats[:, 7:8]),
                inc=[(SE, 1)])
        return SE.n

    def transpose_to_fm(i, src16, dst_fm, mready_se):
        for g in range(4):
            mve_pre = VE.n
            for q in range(4):
                fc = 4 * g + q
                src_sl = src16[:, 128 * fc:128 * (fc + 1)]
                p.emit("tensor", lambda E_, s=src_sl, q=q: E_.transpose(
                    ps_s16[:, 1024 * q:1024 * q + 128], s, idt[:, :]),
                    wait=[(SE, mready_se), (VE, mve_pre)], inc=[(MM, 1)])
            mmm = MM.n
            for q in range(4):
                fc = 4 * g + q
                dst_sl = dst_fm[:, fc, 128 * i:128 * (i + 1)]
                p.emit("vector", lambda E_, d=dst_sl, q=q: E_.tensor_copy(
                    d, ps_s16[:, 1024 * q:1024 * q + 128]),
                    wait=[(MM, mmm)], inc=[(VE, 1)])
        return VE.n

    win_all = view(GC, 0, 16, 768)
    mwv = 0
    for di in range(16):
        mwv = p.dma("gpsimd", win_all[:, di, :], winT[128 * di:128 * (di + 1), :], GP,
                    fence=(di == 0))
    p.dma_slow("gpsimd", binsb[:, 0:6], bin_d.rearrange("(a p) -> p a", p=128), GP)
    mwv = GP.n
    for di in range(2):
        p.dma("gpsimd", wopb[:, di, :], woutT_d[128 * di:128 * (di + 1), :], GP,
              fence=True)
    mwopA = GP.n

    lnfm = view(GA, 0, 16, TOK)
    ln16 = view(GB, 0, 2048)
    for i in range(NT):
        mse = layernorm_tile(i, out16=ln16, mxw=[mx_t[i]])
        transpose_to_fm(i, ln16, lnfm, mse)
    mve_ln = VE.n
    for fc in range(16):
        for t in range(NTH):
            p.dma("sync", ag1c_t[t][128 * fc:128 * (fc + 1), :],
                  lnfm[:, fc, 512 * t:512 * (t + 1)], SY, wait=[(VE, mve_ln)])
    mstore1 = SY.n

    def emit_ag(kind, op, cin, cout, wait):
        p.emit("gpsimd", lambda E_, a=cin, b=cout, kind=kind, op=op: E_.collective_compute(
            kind, op, replica_groups=[list(range(NCORES))], ins=[a], outs=[b]),
            wait=wait, inc=[(CC, 1)])
        return CC.n

    m_ag1 = [emit_ag("AllGather", ALU.bypass, ag1c_t[t][:, :], ag1o_t[t][:, :],
                     [(SY, mstore1)] + ([(GP, GP.n)] if t == 0 else []))
             for t in range(NTH)]
    res_w1, res_w2 = {"m": 0}, {"m": 0}
    wq_units = (wquant_units(w1o_d, FSH, E, w1pc, res_w1) +
                wquant_units(w2o_d, ESH, FF, w2pc, res_w2))

    lnf16 = lnf[:, :].bitcast(F16)
    rhs_b = view(GC, 12288, 2, 4, 512)

    def proj_stream(ago, row_base, n_di, w_fn, do_list, evac, m_w, m_cc, banks=None,
                    bank_war=None):
        if banks is None:
            banks = [ps_s[:, 512 * kk:512 * kk + 512] for kk in range(len(do_list))]
        if not isinstance(m_cc, (list, tuple)):
            m_cc = [m_cc] * NTH
        nb = (n_di + 3) // 4
        batches = [(b, th, bt) for th in range(NTH) for b in range(NCORES)
                   for bt in range(nb)]
        buf_mm = [MM.n, MM.n]
        mld_of = {}
        first = True

        def emit_load(i):
            b, th, bt = batches[i]
            di0 = 4 * bt
            ndi = min(4, n_di - di0)
            buf = i % 2
            sem = SY if buf == 0 else SY2
            src_sl = ago[th][row_base(b) + 128 * di0: row_base(b) + 128 * (di0 + ndi),
                            0:512].rearrange("(a p) c -> p a c", p=128)
            w = [(CC, m_cc[th]), (MM, buf_mm[buf])]
            if i == 1:
                w.append(mld_of[0])
            mld = p.dma("sync", rhs_b[:, buf, 0:ndi, :], src_sl, sem,
                        wait=w, fence=(i == 1))
            return (sem, mld)

        def emit_mms(i):
            b, th, bt = batches[i]
            di0 = 4 * bt
            ndi = min(4, n_di - di0)
            buf = i % 2
            sem, mld = mld_of[i]
            for di in range(di0, di0 + ndi):
                for kk, do in enumerate(do_list):
                    w_sl = w_fn(di, do)
                    waits = [(sem, mld), (GP, m_w)]
                    if di == 0:
                        waits += ([(SE, SE.n), (VE, VE.n)] if bank_war is None
                                  else list(bank_war))
                    p.emit("tensor", lambda E_, w=w_sl, r=rhs_b[:, buf, di - di0, :],
                           o=banks[kk], st_=(di == 0), sp=(di == n_di - 1): E_.matmul(
                        o, w, r, start=st_, stop=sp),
                        wait=waits, inc=[(MM, 1)])
            buf_mm[buf] = MM.n
            if bt == nb - 1:
                mmm = MM.n
                for kk, do in enumerate(do_list):
                    evac(kk, do, b, th, mmm)

        mld_of[0] = (SY, p.dma("sync", rhs_b[:, 0, 0:min(4, n_di), :],
                               ago[0][row_base(0): row_base(0) + 128 * min(4, n_di),
                                      0:512].rearrange("(a p) c -> p a c", p=128), SY,
                               wait=[(CC, m_cc[0]), (MM, MM.n)], fence=True))
        for i in range(len(batches)):
            if i + 1 < len(batches):
                mld_of[i + 1] = emit_load(i + 1)
            emit_mms(i)

    wstr_b2 = view(GB, 12288, 2, 4, 512)

    def proj_local(src_fm, n_di, w_dram, col0, do_list, evac, mln):
        ncol = 128 * len(do_list)
        nb = (n_di + 3) // 4
        batches = [(th, bt) for th in range(NTH) for bt in range(nb)]
        buf_mm = [MM.n, MM.n]
        mld_of = {}

        def emit_load(i):
            th, bt = batches[i]
            di0 = 4 * bt
            ndi = min(4, n_di - di0)
            buf = i % 2
            sem = GP if buf == 0 else GP2
            wsl = w_dram[128 * di0:128 * (di0 + ndi), col0:col0 + ncol].rearrange(
                "(a p) c -> p a c", p=128)
            w = [(MM, buf_mm[buf])]
            if i == 0:
                w.append((VE, mln))
            elif i == 1:
                w.append(mld_of[0])
            mld = p.dma("gpsimd", wstr_b2[:, buf, 0:ndi, 0:ncol], wsl, sem,
                        wait=w, fence=(i <= 1))
            return (sem, mld)

        def emit_mms(i):
            th, bt = batches[i]
            di0 = 4 * bt
            ndi = min(4, n_di - di0)
            buf = i % 2
            sem, mld = mld_of[i]
            for di in range(di0, di0 + ndi):
                for kk in range(len(do_list)):
                    waits = [(sem, mld), (VE, mln)]
                    if di == 0:
                        waits += [(SE, SE.n)]
                    rsl = src_fm[:, di, 512 * th:512 * (th + 1)]
                    p.emit("tensor", lambda E_, w=wstr_b2[:, buf, di - di0, 0:ncol],
                           kk=kk, r=rsl, st_=(di == 0), sp=(di == n_di - 1): E_.matmul(
                        ps_s[:, 512 * kk:512 * kk + 512],
                        w[:, 128 * kk:128 * (kk + 1)], r, start=st_, stop=sp),
                        wait=waits, inc=[(MM, 1)])
            buf_mm[buf] = MM.n
            if bt == nb - 1:
                mmm = MM.n
                for kk, do in enumerate(do_list):
                    evac(kk, do, 0, th, mmm)

        mld_of[0] = emit_load(0)
        for i in range(len(batches)):
            if i + 1 < len(batches):
                mld_of[i + 1] = emit_load(i + 1)
            emit_mms(i)

    esc = lnf16[:, 0:512]
    esc2 = lnf16[:, 512:1024]
    pe_s = lnf16[:, 1024:1536]
    pesw_s = lnf16[:, 1536:2048]

    def norm_update(src16, slot, mready, rows=128, pb0=0):
        sq = esc2 if (rows == 128 and pb0 == 0) else esc2[pb0:pb0 + rows, :]
        s_ = src16
        p.emit("vector", lambda E_, s=s_, q=sq: E_.tensor_mul(q, s, s),
               wait=[mready, (VE, VE.n)], inc=[(VE, 1)])
        mve = VE.n
        oc = onec[:, :] if (rows == 128 and pb0 == 0) else onec[pb0:pb0 + rows, :]
        p.emit("tensor", lambda E_, q=sq, oc=oc: E_.matmul(
            bc_b[0:1, :], oc, q, start=True, stop=True),
            wait=[(VE, mve)], inc=[(MM, 1)])
        p.emit("vector", lambda E_: E_.tensor_reduce(
            sm1[:, 30:31], bc_b[0:1, :], axis=AX.X, op=ALU.max),
            wait=[(MM, MM.n)], inc=[(VE, 1)])
        p.emit("vector", lambda E_, sl=slot: E_.tensor_tensor(
            sm1[:, sl:sl + 1], sm1[:, sl:sl + 1], sm1[:, 30:31], op=ALU.max),
            wait=[(VE, VE.n)], inc=[(VE, 1)])
        return VE.n

    def center_norms(kview, slot, rows=128, extra_wait=()):
        sq16 = view(GC, 0, 2048)
        p.emit("vector", lambda E_, kv=kview: E_.reduce_sum(
            stats[:, 20:21], kv, axis=AX.X),
            wait=[(SE, SE.n), (VE, VE.n), (MM, MM.n)] + list(extra_wait), inc=[(VE, 1)])
        p.emit("vector", lambda E_: E_.tensor_scalar_mul(
            stats[:, 21:22], stats[:, 20:21], 1.0 / S_total),
            wait=[(VE, VE.n)], inc=[(VE, 1)])
        p.emit("vector", lambda E_, kv=kview: E_.tensor_single_scalar(
            kv, kv, stats[:, 21:22], op=ALU.subtract),
            wait=[(VE, VE.n)], inc=[(VE, 1)])
        mcent = VE.n
        oc = onec[:, :] if rows == 128 else onec[0:rows, :]
        for c2 in range(S_total // 2048):
            ksl = kview[:, 2048 * c2:2048 * (c2 + 1)]
            p.emit("vector", lambda E_, a=ksl: E_.tensor_mul(sq16, a, a),
                   wait=[(VE, max(mcent, VE.n)), (MM, MM.n)], inc=[(VE, 1)])
            mve = VE.n
            for q4 in range(4):
                if rows == 128:
                    sq_sl = sq16[:, 512 * q4:512 * (q4 + 1)]
                else:
                    sq_sl = sq16[0:rows, 512 * q4:512 * (q4 + 1)]
                p.emit("tensor", lambda E_, q=sq_sl, oc=oc: E_.matmul(
                    sum_b[0:1, :], oc, q, start=True, stop=True),
                    wait=[(VE, max(mve, VE.n))], inc=[(MM, 1)])
                p.emit("vector", lambda E_: E_.tensor_reduce(
                    sm1[:, 30:31], sum_b[0:1, :], axis=AX.X, op=ALU.max),
                    wait=[(MM, MM.n)], inc=[(VE, 1)])
                p.emit("vector", lambda E_, sl_=slot: E_.tensor_tensor(
                    sm1[:, sl_:sl_ + 1], sm1[:, sl_:sl_ + 1], sm1[:, 30:31], op=ALU.max),
                    wait=[(VE, VE.n)], inc=[(VE, 1)])
        return VE.n

    k_fm = view(GA, 0, 2, S_total)
    v_tm2 = view(GB, 0, NKT, 256)

    def evac_a1(kk, do, b, th, mmm):
        base = b * TOK + 512 * th
        pssl = [ps_s[:, 512 * q:512 * q + 512] for q in range(4)] + [pv_b, sum_b]
        pssl = pssl[kk]
        if do < 2:
            msy_pre = SY.n
            p.emit("scalar", lambda E_, ps_=pssl, bi=binsb[:, do:do + 1]: E_.activation(
                out=esc, in_=ps_, func=AF.Identity, bias=bi),
                wait=[(MM, mmm), (SY, msy_pre), (VE, VE.n)], inc=[(SE, 1)])
            mse = SE.n
            p.dma("sync", qa_dr[128 * do:128 * (do + 1), base:base + 512], esc, SY,
                  wait=[(SE, mse)])
            norm_update(esc, do, (SE, mse))
        elif do < 4:
            h = do - 2
            dst = k_fm[:, h, base:base + 512]
            p.emit("scalar", lambda E_, ps_=pssl, d=dst, bi=binsb[:, do:do + 1]: E_.activation(
                out=d, in_=ps_, func=AF.Identity, bias=bi),
                wait=[(MM, mmm), (SY, mstore1)], inc=[(SE, 1)])
        else:
            h = do - 4
            p.emit("scalar", lambda E_, ps_=pssl, bi=binsb[:, do:do + 1]: E_.activation(
                out=esc2, in_=ps_, func=AF.Identity, bias=bi),
                wait=[(MM, mmm), (VE, VE.n), (SY, SY.n)], inc=[(SE, 1)])
            mse = SE.n
            mve_pre = VE.n
            esc4 = esc2.rearrange("p (a c) -> p a c", a=4, c=128)
            for q in range(4):
                p.emit("tensor", lambda E_, s_=esc4[:, q, :], q=q: E_.transpose(
                    tx16[:, 128 * q:128 * (q + 1)], s_, idt[:, :]),
                    wait=[(SE, mse), (VE, mve_pre)], inc=[(MM, 1)])
            mt = MM.n
            for q in range(4):
                kt = (base + 128 * q) // 128
                dst = v_tm2[:, kt, 128 * h:128 * (h + 1)]
                p.emit("vector", lambda E_, d=dst, q=q: E_.tensor_copy(
                    d, tx16[:, 128 * q:128 * (q + 1)]),
                    wait=[(MM, mt)], inc=[(VE, 1)])

    proj_stream(ag1o_t, lambda b: b * E, 16,
                lambda di, do: win_all[:, di, 128 * do:128 * (do + 1)],
                [0, 1, 2, 3, 4, 5], evac_a1, mwv, m_ag1,
                banks=[ps_s[:, 0:512], ps_s[:, 512:1024], ps_s[:, 1024:1536],
                       ps_s[:, 1536:2048], pv_b, sum_b])

    center_norms(k_fm[:, 0, :], 2)
    center_norms(k_fm[:, 1, :], 3)

    def make_bias(col, slots_q, slots_k, scale):
        p.emit("vector", lambda E_, a=slots_q[0]: E_.tensor_copy(sm1[:, 28:29], sm1[:, a:a + 1]),
               wait=[(VE, VE.n)], inc=[(VE, 1)])
        for a in slots_q[1:]:
            p.emit("vector", lambda E_, a=a: E_.tensor_add(sm1[:, 28:29], sm1[:, 28:29], sm1[:, a:a + 1]),
                   wait=[(VE, VE.n)], inc=[(VE, 1)])
        p.emit("vector", lambda E_, a=slots_k[0]: E_.tensor_copy(sm1[:, 29:30], sm1[:, a:a + 1]),
               wait=[(VE, VE.n)], inc=[(VE, 1)])
        for a in slots_k[1:]:
            p.emit("vector", lambda E_, a=a: E_.tensor_add(sm1[:, 29:30], sm1[:, 29:30], sm1[:, a:a + 1]),
                   wait=[(VE, VE.n)], inc=[(VE, 1)])
        p.emit("vector", lambda E_: E_.tensor_tensor(sm1[:, 28:29], sm1[:, 28:29], sm1[:, 29:30], op=ALU.mult),
               wait=[(VE, VE.n)], inc=[(VE, 1)])
        p.emit("scalar", lambda E_: E_.activation(sm1[:, 28:29], sm1[:, 28:29], AF.Sqrt),
               wait=[(VE, VE.n)], inc=[(SE, 1)])
        p.emit("vector", lambda E_, sc=scale: E_.tensor_scalar(
            sm1[:, 28:29], sm1[:, 28:29], -sc * SAFE, EXPOFF, op0=ALU.mult, op1=ALU.add),
            wait=[(SE, SE.n)], inc=[(VE, 1)])
        mve = VE.n
        mmm_pre = MM.n
        p.emit("tensor", lambda E_: E_.matmul(bc_b[:, 0:1], onerf[:, :], sm1[:, 28:29],
                                              start=True, stop=True),
               wait=[(VE, mve)], inc=[(MM, 1)])
        p.emit("vector", lambda E_, col=col: E_.tensor_copy(bcol[:, col:col + 1], bc_b[:, 0:1]),
               wait=[(MM, MM.n)], inc=[(VE, 1)])
        return VE.n

    make_bias(0, [0], [2], SC_A)
    m_biasA = make_bias(1, [1], [3], SC_A)
    if DEBUG:
        dbg_sm1 = nc.dram_tensor("dbg_sm1", [1, 32], F32, kind="ExternalOutput").ap()
        dbg_bcol = nc.dram_tensor("dbg_bcol", [128, 4], F32, kind="ExternalOutput").ap()
        dbg_kfm = nc.dram_tensor("dbg_kfm", [128, 512], F16, kind="ExternalOutput").ap()
        dbg_q = nc.dram_tensor("dbg_q", [128, 512], F16, kind="ExternalOutput").ap()
        p.dma("sync", dbg_sm1[:, :], sm1[:, :], SY, wait=[(VE, m_biasA)])
        p.dma("sync", dbg_bcol[:, 0:2], bcol[:, 0:2], SY)
        p.dma("sync", dbg_kfm[:, :], k_fm[:, 0, 0:512], SY)
        p.dma("gpsimd", dbg_q[:, :], qa_dr[0:128, 0:512], GP, wait=[(SY, SY.n)])

    BF16 = mybir.dt.bfloat16
    pt = view(GC, 0, 2, 2, 512).bitcast(BF16)
    ptp = [view(GC, 0, 2, 1024).bitcast(BF16)[:, 0, :],
           view(GC, 0, 2, 1024).bitcast(BF16)[:, 1, :]]
    qblk = view(GC, 2048, 2, 2, 512)
    qpeb = view(GC, 12288, 2, 512)
    av_sc = view(GC, 13312, 512)
    kpe_t = view(GC, 4096, S_total)

    dbg_sums = []
    atnB_dr_marker = [None]

    def attention(q_dr_, atn_dr_, scale, bias_cols, v_src, with_pe, m_ready, bg=()):
        bg_state = [0]

        def bg_step(frac):
            tgt = min(len(bg), int(len(bg) * min(1.0, (frac * 4.0 / 3.0) ** 2) + 1e-9))
            while bg_state[0] < tgt:
                bg[bg_state[0]]()
                bg_state[0] += 1
        qpar_mm = [0, 0]
        sexp = [0, 0]
        vmask = [0, 0]
        msump = [0, 0]
        m_ones = [MM.n]
        mld_of = {}
        epi = [None]

        def flush_epi():
            if epi[0] is not None:
                epi[0]()
                epi[0] = None

        def load_q(Qb):
            par = Qb % 2
            sem = SY if par == 0 else SY2
            w = [(MM, qpar_mm[par])] + (list(m_ready) if Qb == 0 else [])
            p.dma("sync", qblk[:, par, 0, :], q_dr_[0:128, 512 * Qb:512 * (Qb + 1)],
                  sem, wait=w, fence=(Qb == 0))
            p.dma("sync", qblk[:, par, 1, :], q_dr_[128:256, 512 * Qb:512 * (Qb + 1)], sem)
            if with_pe:
                p.dma("sync", qpeb[:, par, :], q_dr_[256:384, 512 * Qb:512 * (Qb + 1)], sem)
            return (sem, sem.n)

        mld_of[0] = load_q(0)
        for Qb in range(NQB):
            par_q = Qb % 2
            qsem, mld = mld_of[Qb]
            if Qb + 1 < NQB:
                mld_of[Qb + 1] = load_q(Qb + 1)
            nk = 4 * (Qb + 1)
            for h in range(2):
                ngrp = (nk + 1) // 2
                pend = [None]

                def flush_pv():
                    if pend[0] is not None:
                        pend[0]()
                        pend[0] = None

                for g in range(ngrp):
                    pg = g % 2
                    gt = [j for j in (2 * g, 2 * g + 1) if j < nk]
                    ng = len(gt)
                    for j in gt:
                        pssl = ps_s[:, 1024 * pg + 512 * (j - 2 * g):
                                    1024 * pg + 512 * (j - 2 * g) + 512]
                        ksl = k_fm[:, h, 128 * j:128 * (j + 1)]
                        p.emit("tensor", lambda E_, o=pssl, k=ksl, r=qblk[:, par_q, h, :],
                               sp=(not with_pe): E_.matmul(o, k, r, start=True, stop=sp),
                            wait=[(qsem, mld), (SE, sexp[pg])], inc=[(MM, 1)])
                    if with_pe:
                        for j in gt:
                            pssl = ps_s[:, 1024 * pg + 512 * (j - 2 * g):
                                        1024 * pg + 512 * (j - 2 * g) + 512]
                            kpesl = kpe_t[64 * h:64 * (h + 1), 128 * j:128 * (j + 1)]
                            qpesl = qpeb[64 * h:64 * (h + 1), par_q, :]
                            p.emit("tensor", lambda E_, o=pssl, k=kpesl, r=qpesl: E_.matmul(
                                o, k, r, start=False, stop=True),
                                inc=[(MM, 1)])
                    flush_epi()
                    flush_pv()
                    mgrp = MM.n
                    p.emit("scalar", lambda E_, ng=ng, pg=pg,
                           bc=bcol[:, bias_cols[h]:bias_cols[h] + 1], sc=scale: E_.activation(
                        out=ptp[pg][:, 0:512 * ng], in_=ps_s[:, 1024 * pg:1024 * pg + 512 * ng],
                        func=AF.Exp, bias=bc, scale=sc),
                        wait=[(MM, max(mgrp, msump[pg])), (VE, vmask[pg])], inc=[(SE, 1)])
                    sexp[pg] = SE.n
                    for j in gt:
                        if j >= 4 * Qb:
                            psl = pt[:, pg, j - 2 * g, :]
                            p.emit("vector", lambda E_, a=psl, m=lt[:, j - 4 * Qb, :]: E_.tensor_mul(a, a, m),
                                   wait=[(SE, sexp[pg])], inc=[(VE, 1)])
                    mmask = VE.n
                    for j in gt:
                        psl = pt[:, pg, j - 2 * g, :]
                        if j == 0:
                            p.emit("vector", lambda E_, a=psl: E_.tensor_copy(pacc[:, :], a),
                                   wait=[(SE, sexp[pg]), (VE, mmask), (MM, m_ones[0])],
                                   inc=[(VE, 1)])
                        else:
                            p.emit("vector", lambda E_, a=psl: E_.tensor_tensor(
                                pacc[:, :], pacc[:, :], a, op=ALU.add),
                                wait=[(SE, sexp[pg]), (VE, VE.n)], inc=[(VE, 1)])
                    vmask[pg] = VE.n

                    def mk_pv(gt=gt, pg=pg, g=g, h=h, nk=nk,
                              sexp_v=sexp[pg], mmask_v=mmask):
                        for j in gt:
                            psl = pt[:, pg, j - 2 * g, :]
                            vsl = v_src[:, j, 128 * h:128 * (h + 1)]
                            p.emit("tensor", lambda E_, v=vsl, a=psl, st_=(j == 0),
                                   sp=(j == nk - 1): E_.matmul(
                                pv_b[:, :], v, a, start=st_, stop=sp),
                                wait=[(SE, sexp_v), (VE, mmask_v)], inc=[(MM, 1)])
                        msump[pg] = MM.n
                    pend[0] = mk_pv
                flush_pv()
                qpar_mm[par_q] = MM.n

                def mk_epi(h=h, Qb=Qb, vmx=max(vmask[0], vmask[1])):
                    p.emit("tensor", lambda E_: E_.matmul(sum_b[0:1, :], onecf[:, :], pacc[:, :],
                                                          start=True, stop=True),
                           wait=[(VE, vmx)], inc=[(MM, 1)])
                    m_ones[0] = MM.n
                    mpv = MM.n
                    p.emit("vector", lambda E_: E_.reciprocal(rsum[:, :], sum_b[0:1, :]),
                           wait=[(MM, mpv), (VE, VE.n)], inc=[(VE, 1)])
                    mve = VE.n
                    p.emit("tensor", lambda E_: E_.matmul(bc_b[:, :], onerf[:, :], rsum[:, :],
                                                          start=True, stop=True),
                           wait=[(VE, mve)], inc=[(MM, 1)])
                    p.emit("vector", lambda E_: E_.tensor_copy(rbc[:, :], bc_b[:, :]),
                           wait=[(MM, MM.n)], inc=[(VE, 1)])
                    msy_pre = SY.n
                    p.emit("vector", lambda E_: E_.tensor_tensor(av_sc, pv_b[:, :], rbc[:, :], op=ALU.mult),
                           wait=[(VE, VE.n), (SY, msy_pre)], inc=[(VE, 1)])
                    mve = VE.n
                    p.dma("sync", atn_dr_[128 * h:128 * (h + 1), 512 * Qb:512 * (Qb + 1)],
                          av_sc, SY, wait=[(VE, mve)])
                epi[0] = mk_epi
                bg_step((Qb * 2 + h + 1) / (NQB * 2))
        flush_epi()
        bg_step(1.0)
        return SY.n

    m_atnA = attention(qa_dr, atnA_dr, SC_A, [0, 1], v_tm2, False,
                       [(VE, m_biasA), (MM, MM.n)], bg=wq_units)

    emit_ag("AllGather", ALU.bypass, w1pc[:, :], w1po[:, :], [(SC, res_w1["m"])])
    m_agw = emit_ag("AllGather", ALU.bypass, w2pc[:, :], w2po[:, :], [(SC, res_w2["m"])])

    def out_proj(atn_dr_, w_dram, pXh, m_atn, mw=None):
        wop = wopb
        if mw is None:
            for di in range(2):
                mw = p.dma("gpsimd", wop[:, di, :], w_dram[128 * di:128 * (di + 1), :],
                           GP, wait=[(MM, MM.n)], fence=True)
        atile = view(GC, 4096, 2, 512)
        mstore = [0, 0]
        if TOK >= 1024:
            ts_list = [t_ for t_ in range(NTT) if (512 * t_ % TOK) < TOK // 2] + \
                      [t_ for t_ in range(NTT) if (512 * t_ % TOK) >= TOK // 2]
        else:
            ts_list = list(range(NTT))
        for idx, ts in enumerate(ts_list):
            for di in range(2):
                mmm_pre = MM.n
                src_sl = atn_dr_[128 * di:128 * (di + 1), 512 * ts:512 * (ts + 1)]
                mld = p.dma("sync", atile[:, di, :], src_sl, SY,
                            wait=[(SY, m_atn), (MM, mmm_pre)],
                            fence=(di == 0 and (idx == 0 or idx == NTT // 2)))
            mld = SY.n
            for q in range(4):
                for di in range(2):
                    for dsl in range(4):
                        a_sl = atile[:, di, 128 * q:128 * (q + 1)]
                        w_sl = wop[:, di, 512 * dsl:512 * (dsl + 1)]
                        p.emit("tensor", lambda E_, a=a_sl, w=w_sl, dsl=dsl,
                               st_=(di == 0), sp=(di == 1): E_.matmul(
                            ps_s[:, 512 * dsl:512 * dsl + 512], a, w, start=st_, stop=sp),
                            wait=[(SY, mld), (GP, mw)] + ([(SE, SE.n)] if di == 0 else []),
                            inc=[(MM, 1)])
                mmm = MM.n
                r0 = 512 * ts + 128 * q
                TOK2 = TOK // 2
                hch = (r0 % TOK) // TOK2
                row = (r0 // TOK) * TOK2 + (r0 % TOK2)
                stg4 = lnf16[:, 2048 * (q % 2):2048 * (q % 2) + 2048]
                qeng, qsem = (("sync", SY) if q % 2 == 0 else ("scalar", SC))
                msy_pre = qsem.n
                for dsl in range(4):
                    p.emit("scalar", lambda E_, dsl=dsl, o=stg4[:, 512 * dsl:512 * (dsl + 1)]:
                           E_.activation(out=o, in_=ps_s[:, 512 * dsl:512 * dsl + 512],
                                         func=AF.Identity),
                           wait=[(MM, mmm), (qsem, msy_pre)], inc=[(SE, 1)])
                p.dma(qeng, pXh[hch][row:row + 128, :], stg4, qsem,
                      wait=[(SE, SE.n)], fence=True)
            if idx == NTT // 2 - 1:
                mstore[0] = [(SY, SY.n), (SC, SC.n)]
            mstore[1] = [(SY, SY.n), (SC, SC.n)]
        if TOK < 1024:
            mstore[0] = mstore[1]
        return mstore

    mp2 = out_proj(atnA_dr, woutT_d, p2h, m_atnA, mw=mwopA)
    m_rs2 = [emit_ag("ReduceScatter", ALU.add, p2h[t][:, :], rs2h[t][:, :],
                     mp2[t]) for t in range(2)]

    def mk_residual(rsh, m_rs, with_bias):
        NT2 = NT // 2
        def pre(th):
            halves = [th] if NTH == 2 else [0, 1]
            for h in halves:
                for j in range(NT2):
                    i = h * NT2 + j
                    rt = view(GB, 0, NT, 2048)[:, i, :]
                    mve_pre = VE.n
                    mld = p.dma("sync", rt, rsh[h][128 * j:128 * (j + 1), :], SY,
                                wait=[(CC, m_rs[h]), (VE, mve_pre)], fence=True)
                    if with_bias:
                        p.emit("vector", lambda E_, r=rt: E_.tensor_add(r, r, boutbc[:, :]),
                               wait=[(SY, mld), (VE, m_boutbc)], inc=[(VE, 1)])
                    p.emit("vector", lambda E_, i=i, r=rt: E_.tensor_add(
                        x_tm[:, i, :], x_tm[:, i, :], r),
                        wait=[(SY, mld), (VE, VE.n)], inc=[(VE, 1)])
        return pre

    res2 = mk_residual(rs2h, m_rs2, True)
    if DEBUG:
        dump_x("dbg_x1")

    def ffn_phase(m_w, pre_th=None, ffn_dbg=False):
        xq = view(GA, 0, 16, 512)
        wbuf = view(GA, 8192, 2, 1024)
        q16 = view(GA, 10752, 512)
        stg = view(GA, 12288, 8, 512)

        def hq_tile(hc):
            if hc < 32:
                return view(GB, 512 * hc, 512)
            return view(GC, 512 * (hc - 32), 512)

        for th in range(NTH):
            if pre_th is not None:
                pre_th(th)
            se_buf = [SE.n, SE.n]
            q16_mt = [MM.n, MM.n]
            ve_buf = [VE.n, VE.n]
            for q in range(4):
                i = 4 * th + q
                mse = layernorm_tile(i)
                p.emit("vector", lambda E_: E_.reduce_max(
                    stats[:, 9:10], lnf[:, :], axis=AX.X, apply_absolute_value=True),
                    wait=[(SE, mse), (VE, VE.n)], inc=[(VE, 1)])
                p.emit("vector", lambda E_: E_.tensor_single_scalar(
                    stats[:, 10:11], stats[:, 9:10], 1e-5, op=ALU.max),
                    wait=[(VE, VE.n)], inc=[(VE, 1)])
                p.emit("vector", lambda E_: E_.reciprocal(stats[:, 11:12], stats[:, 10:11]),
                       wait=[(VE, VE.n)], inc=[(VE, 1)])
                p.emit("vector", lambda E_: E_.tensor_scalar_mul(stats[:, 12:13], stats[:, 11:12], 127.0),
                       wait=[(VE, VE.n)], inc=[(VE, 1)])
                p.emit("vector", lambda E_: E_.tensor_scalar_mul(stats[:, 13:14], stats[:, 10:11], 1.0 / 127.0),
                       wait=[(VE, VE.n)], inc=[(VE, 1)])
                mve_s = VE.n
                for sl in range(4):
                    par = sl % 2
                    hm = hmax if par == 0 else rbc
                    fs = fsc if par == 0 else pacc
                    q16b = view(GA, 10752 + 512 * par, 512)
                    txh = tx16[:, 512 * par:512 * par + 512]
                    a = lnf[:, 512 * sl:512 * (sl + 1)]
                    p.emit("vector", lambda E_, a=a, hm=hm: E_.tensor_scalar(
                        hm[:, :], a, stats[:, 12:13], RMAGIC, op0=ALU.mult, op1=ALU.add),
                        wait=[(VE, max(mve_s, VE.n)), (SE, se_buf[par])], inc=[(VE, 1)])
                    p.emit("vector", lambda E_, hm=hm, fs=fs: E_.tensor_scalar_add(
                        fs[:, :], hm[:, :], -RMAGIC),
                        wait=[(VE, VE.n)], inc=[(VE, 1)])
                    mve_q = VE.n
                    p.emit("scalar", lambda E_, fs=fs, o=q16b: E_.activation(
                        out=o, in_=fs[:, :], func=AF.Identity, scale=stats[:, 13:14]),
                        wait=[(VE, mve_q), (MM, q16_mt[par])], inc=[(SE, 1)])
                    se_buf[par] = SE.n
                    mse2 = SE.n
                    q164 = q16b.rearrange("p (a c) -> p a c", a=4, c=128)
                    for qq in range(4):
                        p.emit("tensor", lambda E_, s=q164[:, qq, :], qq=qq, t=txh: E_.transpose(
                            t[:, 128 * qq:128 * (qq + 1)], s, idt[:, :]),
                            wait=[(SE, mse2), (VE, ve_buf[par])], inc=[(MM, 1)])
                    q16_mt[par] = MM.n
                    mt = MM.n
                    for qq in range(4):
                        fc = 4 * sl + qq
                        dst = xq[:, fc, 128 * q:128 * (q + 1)]
                        p.emit("vector", lambda E_, d=dst, qq=qq, t=txh: E_.tensor_copy(
                            d, t[:, 128 * qq:128 * (qq + 1)]),
                            wait=[(MM, mt)], inc=[(VE, 1)])
                    ve_buf[par] = VE.n
            mxq = VE.n
            if DEBUG and ffn_dbg and th == 0:
                dbg_xq = nc.dram_tensor("dbg_xq", [128, 512], F16, kind="ExternalOutput").ap()
                p.dma("sync", dbg_xq[:, :], xq[:, 0, :], SY, wait=[(VE, mxq)])
                p.emit("vector", lambda E_: E_.engine_nop(), wait=[(SY, SY.n)], inc=[(VE, 1)])
            p.emit("vector", lambda E_: E_.memset(hmax[:, :], 0.0),
                   wait=[(VE, VE.n)], inc=[(VE, 1)])
            f1banks = [ps_s[:, 512 * q:512 * (q + 1)] for q in range(4)] + \
                      [ps_p[:, 512 * q:512 * (q + 1)] for q in range(4)]
            buf_mm1 = [MM.n, MM.n]
            for hgp in range(8):
                for di in range(16):
                    buf = di % 2
                    qeng, qsem = (("sync", SY) if (di % 2 == 0) else ("scalar", SC))
                    wsl = w1po[1024 * hgp:1024 * (hgp + 1), 128 * di:128 * (di + 1)]
                    mld = p.dmaT(qeng, wbuf[:, buf, :], wsl, qsem,
                                 wait=[(CC, m_w), (MM, buf_mm1[buf])], fence=True)
                    for hs in range(8):
                        w_sl = wbuf[:, buf, 128 * hs:128 * (hs + 1)]
                        waits = [(qsem, mld), (VE, mxq)]
                        if di == 0:
                            waits += [(SE, SE.n), (VE, VE.n)]
                        p.emit("tensor", lambda E_, w=w_sl, r=xq[:, di, :], o=f1banks[hs],
                               st_=(di == 0), sp=(di == 15): E_.matmul(
                            o, w, r, start=st_, stop=sp),
                            wait=waits, inc=[(MM, 1)])
                    buf_mm1[buf] = MM.n
                mmm = MM.n
                for hs in range(8):
                    hc = 8 * hgp + hs
                    dst = hq_tile(hc)
                    p.emit("scalar", lambda E_, d=dst, o=f1banks[hs]: E_.activation(
                        out=d, in_=o, func=AF.Gelu_apprx_tanh),
                        wait=[(MM, mmm)], inc=[(SE, 1)])
                    p.emit("vector", lambda E_, d=dst: E_.tensor_tensor(
                        rbc[:, :], d, d, op=ALU.mult),
                        wait=[(SE, SE.n), (VE, VE.n)], inc=[(VE, 1)])
                    p.emit("vector", lambda E_: E_.tensor_tensor(
                        hmax[:, :], hmax[:, :], rbc[:, :], op=ALU.max),
                        wait=[(VE, VE.n)], inc=[(VE, 1)])
            mhm = VE.n
            if DEBUG and ffn_dbg and th == 0:
                dbg_h = nc.dram_tensor("dbg_h", [128, 512], F16, kind="ExternalOutput").ap()
                p.dma("sync", dbg_h[:, :], hq_tile(0), SY, wait=[(VE, mhm)])
                dbg_hmax = nc.dram_tensor("dbg_hmax", [128, 512], F32, kind="ExternalOutput").ap()
                p.dma("sync", dbg_hmax[:, :], hmax[:, :], SY)
                p.emit("vector", lambda E_: E_.engine_nop(), wait=[(SY, SY.n)], inc=[(VE, 1)])
            for qq in range(4):
                p.emit("tensor", lambda E_, s=hmax[:, 128 * qq:128 * (qq + 1)], qq=qq: E_.transpose(
                    tx_b[:, 128 * qq:128 * (qq + 1)], s, idtf[:, :]),
                    wait=[(VE, mhm), (VE, VE.n)], inc=[(MM, 1)])
                p.emit("vector", lambda E_, qq=qq: E_.reduce_max(
                    stats[:, 14:15], tx_b[:, 128 * qq:128 * (qq + 1)], axis=AX.X),
                    wait=[(MM, MM.n), (VE, VE.n)], inc=[(VE, 1)])
                p.emit("scalar", lambda E_: E_.activation(
                    stats[:, 14:15], stats[:, 14:15], AF.Sqrt),
                    wait=[(VE, VE.n), (SE, SE.n)], inc=[(SE, 1)])
                p.emit("vector", lambda E_: E_.tensor_single_scalar(
                    stats[:, 15:16], stats[:, 14:15], 1e-5, op=ALU.max),
                    wait=[(VE, VE.n), (SE, SE.n)], inc=[(VE, 1)])
                p.emit("vector", lambda E_: E_.reciprocal(stats[:, 16:17], stats[:, 15:16]),
                       wait=[(VE, VE.n)], inc=[(VE, 1)])
                p.emit("vector", lambda E_: E_.tensor_scalar_mul(
                    stats[:, 17:18], stats[:, 16:17], 127.0),
                    wait=[(VE, VE.n)], inc=[(VE, 1)])
                p.emit("vector", lambda E_: E_.tensor_scalar_mul(
                    stats[:, 18:19], stats[:, 15:16], 1.0 / 127.0),
                    wait=[(VE, VE.n)], inc=[(VE, 1)])
                mcol = VE.n
                for (col, dstf) in ((17, s2f), (18, rs2f)):
                    p.emit("tensor", lambda E_, c=stats[:, col:col + 1]: E_.transpose(
                        tx_b[0:1, 384:512], c, idtf[:, :]),
                        wait=[(VE, max(mcol, VE.n))], inc=[(MM, 1)])
                    p.emit("vector", lambda E_: E_.tensor_copy(sr8[0:1, :], tx_b[0:1, 384:512]),
                           wait=[(MM, MM.n)], inc=[(VE, 1)])
                    mrow = VE.n
                    p.emit("tensor", lambda E_: E_.matmul(
                        bc_b[:, 0:128], onerf[:, :], sr8[0:1, :], start=True, stop=True),
                        wait=[(VE, mrow)], inc=[(MM, 1)])
                    p.emit("vector", lambda E_, d=dstf[:, 128 * qq:128 * (qq + 1)]: E_.tensor_copy(
                        d, bc_b[:, 0:128]),
                        wait=[(MM, MM.n)], inc=[(VE, 1)])
            mvq = VE.n
            if DEBUG and ffn_dbg and th == 0:
                dbg_s2 = nc.dram_tensor("dbg_s2", [128, 512], F32, kind="ExternalOutput").ap()
                dbg_rs2 = nc.dram_tensor("dbg_rs2", [128, 512], F32, kind="ExternalOutput").ap()
                p.dma("sync", dbg_s2[:, :], s2f[:, :], SY, wait=[(VE, mvq)])
                p.dma("sync", dbg_rs2[:, :], rs2f[:, :], SY)
                p.emit("vector", lambda E_: E_.engine_nop(), wait=[(SY, SY.n)], inc=[(VE, 1)])
            mhq_t = {}
            for hc in range(64):
                t_ = hq_tile(hc)
                p.emit("vector", lambda E_, t=t_: E_.tensor_tensor(rbc[:, :], t, s2f[:, :], op=ALU.mult),
                       wait=[(VE, max(mvq, VE.n))], inc=[(VE, 1)])
                p.emit("vector", lambda E_: E_.tensor_scalar_add(hmax[:, :], rbc[:, :], RMAGIC),
                       wait=[(VE, VE.n)], inc=[(VE, 1)])
                p.emit("vector", lambda E_: E_.tensor_scalar_add(rbc[:, :], hmax[:, :], -RMAGIC),
                       wait=[(VE, VE.n)], inc=[(VE, 1)])
                p.emit("vector", lambda E_, t=t_: E_.tensor_tensor(t, rbc[:, :], rs2f[:, :], op=ALU.mult),
                       wait=[(VE, VE.n)], inc=[(VE, 1)])
                mhq_t[hc] = VE.n
            mhq = VE.n
            if DEBUG and ffn_dbg and th == 0:
                dbg_hq = nc.dram_tensor("dbg_hq", [128, 512], F16, kind="ExternalOutput").ap()
                p.dma("sync", dbg_hq[:, :], hq_tile(0), SY, wait=[(VE, mhq)])
                msy = SY.n
                p.emit("vector", lambda E_: E_.engine_nop(), wait=[(SY, msy)], inc=[(VE, 1)])
            buf_mm2 = [MM.n, MM.n]
            m_stg = 0
            for dgp in range(2):
                for hi in range(64):
                    buf = hi % 2
                    qeng, qsem = (("sync", SY) if (hi % 2 == 0) else ("scalar", SC))
                    wsl = w2po[1024 * dgp:1024 * (dgp + 1), 128 * hi:128 * (hi + 1)]
                    mld = p.dmaT(qeng, wbuf[:, buf, :], wsl, qsem,
                                 wait=[(CC, m_w), (MM, buf_mm2[buf])], fence=True)
                    for d8 in range(8):
                        w_sl = wbuf[:, buf, 128 * d8:128 * (d8 + 1)]
                        waits = [(qsem, mld), (VE, max(mhq_t[hi], m_stg))]
                        if hi == 0:
                            waits += [(SE, SE.n)]
                        p.emit("tensor", lambda E_, w=w_sl, r=hq_tile(hi), o=f1banks[d8],
                               st_=(hi == 0), sp=(hi == 63): E_.matmul(
                            o, w, r, start=st_, stop=sp),
                            wait=waits, inc=[(MM, 1)])
                    buf_mm2[buf] = MM.n
                mmm = MM.n
                for d8 in range(8):
                    p.emit("scalar", lambda E_, d=stg[:, d8, :], o=f1banks[d8]: E_.activation(
                        out=d, in_=o, func=AF.Identity),
                        wait=[(MM, mmm), (VE, VE.n)], inc=[(SE, 1)])
                mse = SE.n
                for d8 in range(8):
                    do_c = 8 * dgp + d8
                    mve_pre = VE.n
                    st4 = stg[:, d8, :].rearrange("p (a c) -> p a c", a=4, c=128)
                    for tq in range(4):
                        p.emit("tensor", lambda E_, s_=st4[:, tq, :], tq=tq: E_.transpose(
                            tx16[:, 128 * tq:128 * (tq + 1)], s_, idt[:, :]),
                            wait=[(SE, mse), (VE, mve_pre)], inc=[(MM, 1)])
                    mt = MM.n
                    for tq in range(4):
                        i = 4 * th + tq
                        dst = x_tm[:, i, 128 * do_c:128 * (do_c + 1)]
                        p.emit("vector", lambda E_, d=dst, tq=tq: E_.tensor_tensor(
                            d, d, tx16[:, 128 * tq:128 * (tq + 1)], op=ALU.add),
                            wait=[(MM, mt)], inc=[(VE, 1)])
                m_stg = VE.n
        return VE.n

    ffn_phase(m_agw, pre_th=res2, ffn_dbg=True)
    if DEBUG:
        dump_x("dbg_x2")
        dbg_w1p = nc.dram_tensor("dbg_w1p", [128, 2048], F16, kind="ExternalOutput").ap()
        p.dma("sync", dbg_w1p[:, :], w1po[0:128, :], SY, wait=[(CC, m_agw)])

    ln3fm = view(GA, 0, 16, TOK)
    ln16b = view(GB, 4096, 2048)
    for i in range(NT):
        mse = layernorm_tile(i, out16=ln16b)
        transpose_to_fm(i, ln16b, ln3fm, mse)
    mln3 = VE.n

    p.emit("gpsimd", lambda E_, o=gainsb[:, 0:12],
           i=qaln_d.rearrange("(a p) -> p a", p=128): E_.dma_start(
        out=o, in_=i, allow_slow_non_contiguous=True),
        wait=[(GP, GP.n)], inc=[(GP, 16)])
    p.emit("gpsimd", lambda E_, o=gainsb[:, 12:16],
           i=kvln_d.rearrange("(a p) -> p a", p=128): E_.dma_start(
        out=o, in_=i, allow_slow_non_contiguous=True),
        wait=[(GP, GP.n)], inc=[(GP, 16)])
    mgain = GP.n

    qa_s = view(GB, 0, 12, TOK)
    def mk_evac_store(dst_fm):
        def ev(kk, do, b, th, mmm):
            dsl = dst_fm[:, do, 512 * th:512 * (th + 1)]
            p.emit("scalar", lambda E_, d=dsl, kk=kk: E_.activation(
                out=d, in_=ps_s[:, 512 * kk:512 * kk + 512], func=AF.Identity),
                wait=[(MM, mmm)], inc=[(SE, 1)])
        return ev

    proj_local(ln3fm, 16, wqa_d, 0, [0, 1, 2, 3], mk_evac_store(qa_s), mln3)
    proj_local(ln3fm, 16, wqa_d, 512, [4, 5, 6, 7], mk_evac_store(qa_s), mln3)
    proj_local(ln3fm, 16, wqa_d, 1024, [8, 9, 10, 11], mk_evac_store(qa_s), mln3)
    mqa = SE.n

    ckv_s = view(GC, 4096, 4, TOK)
    kpe_l = view(GC, 8192, TOK)
    kpesw_l = view(GC, 9216, TOK)

    def ev_kva(kk, do, b, th, mmm):
        if do < 4:
            dsl = ckv_s[:, do, 512 * th:512 * (th + 1)]
            p.emit("scalar", lambda E_, d=dsl, kk=kk: E_.activation(
                out=d, in_=ps_s[:, 512 * kk:512 * kk + 512], func=AF.Identity),
                wait=[(MM, mmm)], inc=[(SE, 1)])
        else:
            dsl = (kpe_l if do == 4 else kpesw_l)[0:64, 512 * th:512 * (th + 1)]
            p.emit("scalar", lambda E_, d=dsl, kk=kk: E_.activation(
                out=d, in_=ps_s[0:64, 512 * kk:512 * kk + 512], func=AF.Identity),
                wait=[(MM, mmm)], inc=[(SE, 1)])

    proj_local(ln3fm, 16, wkva_d, 0, [0, 1, 2, 3], ev_kva, mln3)
    for th in range(NTH):
        for bt in range(4):
            buf = bt % 2
            mmm_pre = MM.n
            di0 = 4 * bt
            wsl = wkva_d[128 * di0:128 * (di0 + 4), 512:640].rearrange(
                "(a p) c -> p a c", p=128)
            dbuf = wstr_b2[:, buf, 0:4, 0:128]
            mld = p.dma("gpsimd", dbuf, wsl, GP, wait=[(MM, mmm_pre)],
                        fence=(th == 0 and bt == 0))
            for di in range(di0, di0 + 4):
                for kk in range(2):
                    waits = [(GP, mld), (VE, mln3)]
                    if di == 0:
                        waits += [(SE, SE.n)]
                    rsl = ln3fm[:, di, 512 * th:512 * (th + 1)]
                    p.emit("tensor", lambda E_, w=dbuf[:, di - di0, :], kk=kk, r=rsl,
                           st_=(di == 0), sp=(di == 15): E_.matmul(
                        ps_s[0:64, 512 * kk:512 * kk + 512],
                        w[:, 64 * kk:64 * (kk + 1)], r, start=st_, stop=sp),
                        wait=waits, inc=[(MM, 1)])
        mmm = MM.n
        for kk in range(2):
            ev_kva(kk, 4 + kk, 0, th, mmm)
    mkva = SE.n

    def rms_apply(src_fm, nchunk, gain_off, dst_rows, eps, mproj, denom):
        for th in range(NTH):
            mve_pre = VE.n
            for ch in range(nchunk):
                ssl = src_fm[:, ch, 512 * th:512 * (th + 1)]
                p.emit("vector", lambda E_, s=ssl: E_.tensor_mul(esc2, s, s),
                       wait=[(SE, mproj), (MM, MM.n),
                             (VE, mve_pre if ch == 0 else VE.n)], inc=[(VE, 1)])
                mve = VE.n
                p.emit("tensor", lambda E_, st_=(ch == 0), sp=(ch == nchunk - 1): E_.matmul(
                    sum_b[0:1, :], onec[:, :], esc2, start=st_, stop=sp),
                    wait=[(VE, mve)], inc=[(MM, 1)])
            mmm = MM.n
            p.emit("vector", lambda E_, d=denom: E_.tensor_scalar(
                rsum[:, :], sum_b[0:1, :], 1.0 / d, eps, op0=ALU.mult, op1=ALU.add),
                wait=[(MM, mmm), (VE, VE.n)], inc=[(VE, 1)])
            p.emit("scalar", lambda E_: E_.activation(rsum[:, :], rsum[:, :], AF.Sqrt),
                   wait=[(VE, VE.n)], inc=[(SE, 1)])
            p.emit("vector", lambda E_: E_.reciprocal(rsum[:, :], rsum[:, :]),
                   wait=[(SE, SE.n)], inc=[(VE, 1)])
            mve = VE.n
            p.emit("tensor", lambda E_: E_.matmul(bc_b[:, :], onerf[:, :], rsum[:, :],
                                                  start=True, stop=True),
                   wait=[(VE, mve)], inc=[(MM, 1)])
            p.emit("vector", lambda E_: E_.tensor_copy(rbc[:, :], bc_b[:, :]),
                   wait=[(MM, MM.n)], inc=[(VE, 1)])
            mrbc = VE.n
            for ch in range(nchunk):
                ssl = src_fm[:, ch, 512 * th:512 * (th + 1)]
                msy_pre = SY.n
                p.emit("vector", lambda E_, s=ssl, g=gainsb[:, gain_off + ch:gain_off + ch + 1]:
                       E_.scalar_tensor_tensor(esc, s, g, rbc[:, :], op0=ALU.mult, op1=ALU.mult),
                       wait=[(VE, max(mrbc, VE.n)), (SY, msy_pre), (GP, mgain)], inc=[(VE, 1)])
                p.dma("sync", ag3c_t[th][dst_rows + 128 * ch:dst_rows + 128 * (ch + 1),
                                         0:512], esc, SY,
                      wait=[(VE, VE.n)])
        return SY.n

    rms_apply(qa_s, 12, 0, 0, 1e-6, mqa, QL)
    mag3a = rms_apply(ckv_s, 4, 12, QL, 1e-6, mkva, KL)
    mag3_t = []

    cosk_sb = view(GB, 12288, TOK)
    sink_sb = view(GB, 13312, TOK)
    p.dma("gpsimd", cosk_sb[0:64, :], cosk_d[:, :], GP,
          wait=[(MM, MM.n), (VE, VE.n), (SY, SY.n)])
    p.dma("gpsimd", sink_sb[0:64, :], sink_d[:, :], GP)
    mck = GP.n
    for th in range(NTH):
        sl = slice(512 * th, 512 * (th + 1))
        a_ = kpe_l[0:64, sl]
        b_ = kpesw_l[0:64, sl]
        c_ = cosk_sb[0:64, sl]
        d_ = sink_sb[0:64, sl]
        e1 = esc[0:64, :]
        e2 = esc2[0:64, :]
        p.emit("vector", lambda E_, a=a_, c=c_, e=e1: E_.tensor_tensor(e, a, c, op=ALU.mult),
               wait=[(SE, mkva), (GP, mck), (VE, VE.n), (SY, SY.n)], inc=[(VE, 1)])
        p.emit("vector", lambda E_, b=b_, d=d_, e=e2: E_.tensor_tensor(e, b, d, op=ALU.mult),
               wait=[(VE, VE.n)], inc=[(VE, 1)])
        p.emit("vector", lambda E_, e=e1, f=e2: E_.tensor_tensor(e, e, f, op=ALU.add),
               wait=[(VE, VE.n)], inc=[(VE, 1)])
        mve = VE.n
        mag3_t.append(p.dma("sync", ag3c_t[th][QL + KL:QL + KL + 64, 0:512], e1, SY,
                            wait=[(VE, mve)], fence=True))
    m_ag3 = [emit_ag("AllGather", ALU.bypass, ag3c_t[t][:, :], ag3o_t[t][:, :],
                     [(SY, mag3_t[t])])
             for t in range(NTH)]

    wqb_sb = view(GB, 0, 12, 512)
    _kvb16a = s2f[:, :].bitcast(F16)
    _kvb16b = rs2f[:, :].bitcast(F16)

    def wkvb_tile(di):
        g = _kvb16a if di < 2 else _kvb16b
        return g[:, 512 * (di % 2):512 * (di % 2 + 1)]
    mwb = 0
    for di in range(12):
        mwb = p.dma("gpsimd", wqb_sb[:, di, :], wqb_d[128 * di:128 * (di + 1), :], GP,
                    wait=[(VE, VE.n)], fence=(di == 0))
    for di in range(4):
        mwb = p.dma("gpsimd", wkvb_tile(di), wkvb_d[128 * di:128 * (di + 1), :], GP,
                    wait=[(VE, VE.n)])

    cq_sb = view(GB, 12288, 512)
    sq_sb = view(GB, 12800, 512)

    def ev_qb(kk, do, b, th, mmm):
        base = b * TOK + 512 * th
        pssl = ps_s[:, 512 * kk:512 * kk + 512]
        if do < 2:
            msy_pre = SY.n
            p.emit("scalar", lambda E_, ps_=pssl: E_.activation(
                out=esc, in_=ps_, func=AF.Identity),
                wait=[(MM, mmm), (SY, msy_pre), (VE, VE.n)], inc=[(SE, 1)])
            mse = SE.n
            p.dma("sync", qb_dr[128 * do:128 * (do + 1), base:base + 512], esc, SY,
                  wait=[(SE, mse)])
            norm_update(esc, 4 + do, (SE, mse))
        elif do == 2:
            p.emit("scalar", lambda E_, ps_=pssl: E_.activation(
                out=pe_s, in_=ps_, func=AF.Identity),
                wait=[(MM, mmm), (VE, VE.n)], inc=[(SE, 1)])
        else:
            p.emit("scalar", lambda E_, ps_=pssl: E_.activation(
                out=pesw_s, in_=ps_, func=AF.Identity),
                wait=[(MM, mmm)], inc=[(SE, 1)])
            mse = SE.n
            mgq = p.dma("gpsimd", cq_sb, cosq_d[:, base:base + 512], GP,
                        wait=[(VE, VE.n), (MM, MM.n)])
            mgq = p.dma("gpsimd", sq_sb, sinq_d[:, base:base + 512], GP)
            p.emit("vector", lambda E_: E_.tensor_tensor(esc, pe_s, cq_sb, op=ALU.mult),
                   wait=[(SE, mse), (GP, mgq), (VE, VE.n), (SY, SY.n)], inc=[(VE, 1)])
            p.emit("vector", lambda E_: E_.tensor_tensor(esc2, pesw_s, sq_sb, op=ALU.mult),
                   wait=[(VE, VE.n)], inc=[(VE, 1)])
            msy_pre = SY.n
            p.emit("vector", lambda E_: E_.tensor_tensor(esc, esc, esc2, op=ALU.add),
                   wait=[(VE, VE.n), (SY, msy_pre)], inc=[(VE, 1)])
            mve = VE.n
            p.dma("sync", qb_dr[256:384, base:base + 512], esc, SY, wait=[(VE, mve)])
            norm_update(esc[0:64, :], 6, (VE, mve), rows=64, pb0=0)
            norm_update(esc[64:128, :], 7, (VE, VE.n), rows=64, pb0=64)
        return

    proj_stream(ag3o_t, lambda b: b * AG3R, 12,
                lambda di, do: wqb_sb[:, di, 128 * do:128 * (do + 1)],
                [0, 1, 2, 3], ev_qb, mwb, m_ag3)

    def ev_kvb(kk, do, b, th, mmm):
        base = b * TOK + 512 * th
        pssl = ps_s[:, 512 * kk:512 * kk + 512]
        if do < 2:
            h = do
            dst = k_fm[:, h, base:base + 512]
            p.emit("scalar", lambda E_, ps_=pssl, d=dst: E_.activation(
                out=d, in_=ps_, func=AF.Identity),
                wait=[(MM, mmm)], inc=[(SE, 1)])
        else:
            h = do - 2
            pssl = ps_s[:, 512 * kk:512 * kk + 512]
            base = b * TOK + 512 * th
            p.emit("scalar", lambda E_, ps_=pssl: E_.activation(
                out=esc2, in_=ps_, func=AF.Identity),
                wait=[(MM, mmm), (VE, VE.n), (SY, SY.n)], inc=[(SE, 1)])
            mse = SE.n
            mve_pre = VE.n
            esc4 = esc2.rearrange("p (a c) -> p a c", a=4, c=128)
            for q in range(4):
                p.emit("tensor", lambda E_, s_=esc4[:, q, :], q=q: E_.transpose(
                    tx16[:, 128 * q:128 * (q + 1)], s_, idt[:, :]),
                    wait=[(SE, mse), (VE, mve_pre)], inc=[(MM, 1)])
            mt = MM.n
            for q in range(4):
                kt = (base + 128 * q) // 128
                dst = v_tm2[:, kt, 128 * h:128 * (h + 1)]
                p.emit("vector", lambda E_, d=dst, q=q: E_.tensor_copy(
                    d, tx16[:, 128 * q:128 * (q + 1)]),
                    wait=[(MM, mt)], inc=[(VE, 1)])

    proj_stream(ag3o_t, lambda b: b * AG3R + QL, 4,
                lambda di, do: wkvb_tile(di)[:, 128 * do:128 * (do + 1)],
                [0, 1, 2, 3], ev_kvb, mwb, m_ag3)

    mkpe = 0
    first = True
    for b in range(NCORES):
        for t in range(NTH):
            src = ag3o_t[t][b * AG3R + QL + KL: b * AG3R + QL + KL + 64, :]
            cols = slice(b * TOK + 512 * t, b * TOK + 512 * (t + 1))
            w = [(MM, MM.n), (VE, VE.n), (SY, SY.n), (CC, m_ag3[-1])] if first else []
            first = False
            mkpe = p.dma("sync", kpe_t[0:64, cols], src, SY, wait=w)
            mkpe = p.dma("sync", kpe_t[64:128, cols], src, SY)

    center_norms(k_fm[:, 0, :], 8)
    center_norms(k_fm[:, 1, :], 9)
    center_norms(kpe_t[:, :], 10, rows=64, extra_wait=[(SY, mkpe)])

    make_bias(2, [4, 6], [8, 10], SC_B)
    m_biasB = make_bias(3, [5, 7], [9, 10], SC_B)
    if DEBUG:
        dbg2_sm1 = nc.dram_tensor("dbg2_sm1", [1, 32], F32, kind="ExternalOutput").ap()
        dbg2_bcol = nc.dram_tensor("dbg2_bcol", [128, 4], F32, kind="ExternalOutput").ap()
        dbg_kb = nc.dram_tensor("dbg_kb", [128, 512], F16, kind="ExternalOutput").ap()
        dbg_qb = nc.dram_tensor("dbg_qb", [128, 512], F16, kind="ExternalOutput").ap()
        dbg_kpe = nc.dram_tensor("dbg_kpe", [128, 512], F16, kind="ExternalOutput").ap()
        dbg_qpe = nc.dram_tensor("dbg_qpe", [128, 512], F16, kind="ExternalOutput").ap()
        p.dma("sync", dbg2_sm1[:, :], sm1[:, :], SY, wait=[(VE, m_biasB)])
        p.dma("sync", dbg2_bcol[:, :], bcol[:, :], SY)
        p.dma("sync", dbg_kb[:, :], k_fm[:, 0, 0:512], SY)
        p.dma("sync", dbg_kpe[:, :], kpe_t[:, 0:512], SY)
        p.dma("gpsimd", dbg_qb[:, :], qb_dr[0:128, 0:512], GP, wait=[(SY, SY.n)])
        p.dma("gpsimd", dbg_qpe[:, :], qb_dr[256:384, 0:512], GP)

    atnB_dr_marker[0] = atnB_dr
    m_atnB = attention(qb_dr, atnB_dr, SC_B, [2, 3], v_tm2, True,
                       [(VE, m_biasB), (MM, MM.n)])

    mp4 = out_proj(atnB_dr, woutb_d, p4h, m_atnB)
    m_rs4 = [emit_ag("ReduceScatter", ALU.add, p4h[t][:, :], rs4h[t][:, :],
                     mp4[t]) for t in range(2)]
    res4 = mk_residual(rs4h, m_rs4, False)
    if DEBUG:
        dump_x("dbg_x3")

    ffn_phase(m_agw, pre_th=res4)

    mfin = 0
    for i in range(NT):
        mve_pre = VE.n
        mfin = p.dma("sync", y_out[128 * i:128 * (i + 1), :], x_tm[:, i, :], SY,
                     wait=[(VE, mve_pre)])
    p.emit("gpsimd", lambda E_: E_.engine_nop(), wait=[(SY, mfin)])

    p.finalize()
    return nc


_NC_CACHE = {}


def kernel(**inputs):
    S_total = int(np.asarray(inputs["L0_attn_a_hidden_states"]).shape[1])
    if S_total not in _NC_CACHE:
        _NC_CACHE[S_total] = build(S_total)
    nc = _NC_CACHE[S_total]
    maps = host_inputs(inputs, S_total)
    res = run_bass_kernel_spmd(nc, maps, core_ids=list(range(NCORES)))
    out = np.concatenate([res.results[c]["y"] for c in range(NCORES)], axis=0)
    return out.reshape(1, S_total, E).astype(np.float32)
